# version 33
# baseline (speedup 1.0000x reference)
import numpy as np
import ml_dtypes
import concourse.bass as bass
import concourse.mybir as mybir
from concourse.bass_utils import run_bass_kernel_spmd

F32 = mybir.dt.float32
BF16 = mybir.dt.bfloat16
Alu = mybir.AluOpType
Act = mybir.ActivationFunctionType
AXX = mybir.AxisListType.X

T, C = 2048, 1024
NCORES = 8

_DEBUG_TAPS = ()
_DEFER_SCHED = {-1: [("q", 0), ("q", 1)],
                0: [("wd", wt) for wt in range(6)],
                1: [("wd", 6), ("wd", 7)] + [("q", t) for t in range(2, 6)],
                2: [("q", t) for t in range(6, 12)],
                3: [("q", t) for t in range(12, 16)]}
_MAX_PHASE = 99
_MAX_GROUPS = 4
_P2_SUB = 9


def _swap_perm():
    p = np.arange(32)
    p[0::2] += 1
    p[1::2] -= 1
    return p


def _host_consts():
    bf = ml_dtypes.bfloat16
    t = np.arange(T, dtype=np.float64)
    inv = 1.0 / (10000.0 ** (np.arange(0, 32, 2, dtype=np.float64) / 32))
    emb = t[:, None] * np.concatenate([inv, inv])[None, :]
    cos = np.cos(emb).T
    sin_fold = np.sin(emb).T.copy()
    sin_fold[0::2, :] *= -1.0
    ones32 = np.ones((32, T))
    zer32 = np.zeros((32, T))
    cos4 = np.concatenate([ones32, cos, ones32, cos], axis=0).astype(bf)
    sin4 = np.concatenate([zer32, sin_fold, zer32, sin_fold], axis=0).astype(bf)
    j = np.arange(128)[:, None]
    i = np.arange(512)[None, :]
    mask4 = np.concatenate(
        [np.tile((i >= j + 128 * m).astype(np.float32), (1, 2))
         for m in range(4)], axis=1).astype(bf)
    ph = np.zeros((80, 8 * 128), dtype=np.float32)
    for p in range(8):
        ph[64 + 2 * p, 128 * p:128 * p + 64] = 1.0
        ph[64 + 2 * p + 1, 128 * p + 64:128 * p + 128] = 1.0
    idn = np.eye(128, dtype=np.float32)
    return {"cos4": cos4, "sin4": sin4, "mask4": mask4,
            "pairhot": ph, "idn32": idn}


def _layout_inputs(inputs):
    consts = _host_consts()
    sp = _swap_perm()
    Wq_up = np.concatenate(
        [np.asarray(inputs["L0_attn_Wq_up_c"]), np.asarray(inputs["L0_attn_Wq_up_e"]),
         np.asarray(inputs["L0_attn_Wq_up_e"])[:, sp]], axis=1).astype(np.float32)
    Wk_up = np.concatenate(
        [np.asarray(inputs["L0_attn_Wk_up_c"]), np.asarray(inputs["L0_attn_Wk_up_e"]),
         np.asarray(inputs["L0_attn_Wk_up_e"])[:, sp]], axis=1).astype(np.float32)
    biascol = np.ascontiguousarray(
        np.asarray(inputs["L0_ffn_bias"], dtype=np.float32).reshape(8, 128).T)
    x = np.ascontiguousarray(np.asarray(inputs["L0_attn_x"], dtype=np.float32))
    shared = {
        "wq_d": np.ascontiguousarray(np.asarray(inputs["L0_attn_Wq_down"], dtype=np.float32)),
        "wk_d": np.ascontiguousarray(np.asarray(inputs["L0_attn_Wk_down"], dtype=np.float32)),
        "wv_d": np.ascontiguousarray(np.asarray(inputs["L0_attn_Wv_down"], dtype=np.float32)),
        "wq_up": Wq_up, "wk_up": Wk_up,
        "wv_up": np.ascontiguousarray(np.asarray(inputs["L0_attn_Wv_up"], dtype=np.float32)),
        "wc": np.ascontiguousarray(np.asarray(inputs["L0_attn_Wc"], dtype=np.float32)),
        "wffn": np.ascontiguousarray(np.asarray(inputs["L0_ffn_weight"], dtype=np.float32)),
        "biascol": biascol,
        **consts,
    }
    return [dict(shared, x=x[b]) for b in range(NCORES)]


ENGS = ("sync", "gp", "v", "a", "t")


class Plan:
    def __init__(self):
        self.streams = {e: [] for e in ENGS}
        self.count = {}

    def add(self, eng, emit, wait=(), inc=None, n=1):
        tok = None
        if inc is not None:
            self.count[inc] = self.count.get(inc, 0) + n
            tok = (inc, self.count[inc])
        waits = tuple(w for w in wait if w is not None)
        self.streams[eng].append((waits, emit, inc, n))
        return tok

    def tok(self, sem):
        return (sem, self.count.get(sem, 0))

    def realize(self, eng_name, eng, sems):
        hi = {}
        for waits, emit, inc, n in self.streams[eng_name]:
            for s, v in waits:
                if v > hi.get(s, 0):
                    eng.wait_ge(sems[s], v)
                    hi[s] = v
            inst = emit(eng)
            if inc is not None:
                if inst is not None:
                    inst.then_inc(sems[inc], n)
                else:
                    eng.sem_inc(sems[inc], n)


def _mx(*toks):
    best = {}
    for t_ in toks:
        if t_ is None:
            continue
        if isinstance(t_, list):
            for s, v in t_:
                if v > best.get(s, 0):
                    best[s] = v
        else:
            s, v = t_
            if v > best.get(s, 0):
                best[s] = v
    return [(s, v) for s, v in best.items()]


_LAST_PLAN = None


def build(taps=()):
    global _LAST_PLAN
    nc = bass.Bass("TRN2", target_bir_lowering=False)
    P = Plan()
    _LAST_PLAN = P

    def dram(name, shape, dt=F32, kind="ExternalInput"):
        return nc.dram_tensor(name, list(shape), dt, kind=kind).ap()

    x_h = dram("x", (T, C))
    W_H = (dram("wq_d", (C, C)), dram("wk_d", (C, C)), dram("wv_d", (C, C)))
    wqu_h = dram("wq_up", (64, 96))
    wku_h = dram("wk_up", (64, 96))
    wvu_h = dram("wv_up", (64, 64))
    wc_h = dram("wc", (C, C))
    wffn_h = dram("wffn", (C, C))
    bias_h = dram("biascol", (128, 8))
    cos4_h = dram("cos4", (128, T), BF16)
    sin4_h = dram("sin4", (128, T), BF16)
    mask4_h = dram("mask4", (128, 4096), BF16)
    ph_h = dram("pairhot", (80, 1024))
    idn_h = dram("idn32", (128, 128))
    out_h = dram("out", (T, C), kind="ExternalOutput")
    sk = lambda n: "ExternalOutput" if n in taps else "Internal"
    xn_s = dram("xn_s", (T, C), BF16, kind=sk("xn_s"))
    xd_s = dram("xd_s", (T, C), BF16, kind=sk("xd_s"))
    wd_s = dram("wd_s", (C, C), BF16, kind=sk("wd_s"))
    y_s = dram("y_s", (C, T), BF16, kind=sk("y_s"))
    ffn_s = dram("ffn_s", (C, T), BF16, kind=sk("ffn_s"))

    cms = []

    def ent(cm):
        cms.append(cm)
        return cm.__enter__()

    def sb(name, shape, dt=F32):
        return ent(nc.sbuf_tensor(name, list(shape), dt))

    psA = ent(nc.psum_tensor("psA", [128, 2048], F32))
    psB = ent(nc.psum_tensor("psB", [128, 2048], F32))
    c_cos4 = sb("c_cos4", (128, T), BF16)
    c_sin4 = sb("c_sin4", (128, T), BF16)
    c_mask = sb("c_mask", (128, 4096), BF16)
    c_idn = sb("c_idn", (128, 128), F32)
    c_bias = sb("c_bias", (128, 8), F32)
    x_ring = sb("x_ring", (128, 3 * C), F32)
    dbuf = sb("dbuf", (80, T), F32)
    p_ring = sb("p_ring", (128, 2048), BF16)
    p_m = sb("p_m", (128, 2048), BF16)

    cmsB = []

    def sbB(name, shape, dt=F32):
        cm = nc.sbuf_tensor(name, list(shape), dt)
        cmsB.append(cm)
        return cm.__enter__()

    wf_ring = sbB("wf_ring", (128, 2 * C), F32)
    wd_ring = sbB("wd_ring", (128, C), BF16)
    xnf_ring = sbB("xnf_ring", (128, 2 * C), F32)
    xnb_ring = sbB("xnb_ring", (128, 2 * C), BF16)
    xdb_ring = sbB("xdb_ring", (128, 2 * C), BF16)
    qb = [sbB(f"qb{i}", (128, 2 * 512), F32) for i in range(6)]
    bm = [sbB(f"bm{i}", (128, 2 * 16), F32) for i in range(3)]
    stats = {k: sbB("st_" + k, (128, 16), F32)
             for k in ("su", "sq", "mean", "ex2", "m2", "var", "ve", "sd",
                       "r0", "t1", "t2", "t3", "rstd", "nmr")}
    wupq = sbB("wupq", (128, 96), BF16)
    wupq_s = sbB("wupq_s", (128, 96), BF16)
    wupk = sbB("wupk", (128, 96), BF16)
    wupv = sbB("wupv", (128, 64), BF16)
    wg_bf = [sbB(f"wg{w}", (128, 2 * 2048), BF16) for w in range(3)]
    xnT = sbB("xnT", (128, 8 * T), BF16)
    lat = [sbB(f"lat{w}", (128, 2 * T), BF16) for w in range(3)]
    qAB = [sbB(f"qAB{p}", (128, T), BF16) for p in range(2)]
    kAB = [sbB(f"kAB{p}", (128, T), BF16) for p in range(2)]
    v_ext = [sbB(f"vext{j}", (128, 16 * 80), BF16) for j in range(4)]
    rp1 = sbB("rp1", (128, 512), BF16)
    rp2 = sbB("rp2", (128, 512), BF16)
    ystage = sbB("ystage", (80, 2 * 512), BF16)

    ld_consts = []
    for dst, src in ((c_cos4, cos4_h), (c_sin4, sin4_h), (c_mask, mask4_h),
                     (c_idn, idn_h), (c_bias, bias_h)):
        ld_consts.append(P.add("sync", lambda e, d=dst, s=src:
                               e.dma_start(out=d[:], in_=s), inc="S_CONST", n=16))
    t_consts = _mx(*ld_consts)
    gp_up = []
    for dst, srch in ((wupq, wqu_h), (wupk, wku_h), (wupv, wvu_h)):
        for hh in range(2):
            gp_up.append(P.add("gp", lambda e, d=dst, s=srch, hh=hh:
                               e.dma_start(out=d[64 * hh:64 * (hh + 1), :], in_=s),
                               inc="GD", n=16))
    tq_s = P.add("v", lambda e: e.tensor_scalar(
        out=wupq_s[:], in0=wupq[:], scalar1=0.125, scalar2=0.0,
        op0=Alu.mult, op1=Alu.bypass), wait=_mx(*gp_up), inc="V")
    tok_up = _mx(tq_s, *gp_up)
    tok_dbuf_init = P.add("v", lambda e: e.memset(dbuf[64:80, :], 0.0), inc="V")
    tok_ps_init = P.add("v", lambda e: (e.memset(psA[:], 0.0),
                                        e.memset(psB[:], 0.0))[-1], inc="V")

    qset_prev = [None, None]

    def emit_quant(src_ap, dst_ap, set_i, src_wait, dst_wait, out_tok):
        b = [q[:, 512 * set_i:512 * (set_i + 1)] for q in qb]
        bmx = [m[:, 16 * set_i:16 * (set_i + 1)] for m in bm]
        seg = src_ap.rearrange("p (nb s) -> p nb s", s=32)
        w0 = _mx(*src_wait, qset_prev[set_i])
        tV = P.add("v", lambda e, bmx=bmx, seg=seg: e.tensor_reduce(
            out=bmx[0], in_=seg, axis=AXX, op=Alu.max,
            apply_absolute_value=True), wait=w0, inc="V")
        yield
        tV = P.add("v", lambda e, bmx=bmx: e.tensor_scalar(
            out=bmx[1], in0=bmx[0], scalar1=1.0 / 60.0, scalar2=1.25e-31,
            op0=Alu.mult, op1=Alu.max), wait=[tV], inc="V")
        yield
        tV = P.add("v", lambda e, bmx=bmx: e.reciprocal(out=bmx[2], in_=bmx[1]),
                   wait=[tV], inc="V")
        yield
        rsb = bmx[2].rearrange("p nb -> p nb ()").broadcast_to((128, 16, 32))
        s8b = bmx[1].rearrange("p nb -> p nb ()").broadcast_to((128, 16, 32))
        tY = P.add("gp", lambda e, b=b, seg=seg, rsb=rsb: e.tensor_tensor(
            out=b[0].rearrange("p (nb s) -> p nb s", s=32), in0=seg, in1=rsb,
            op=Alu.mult), wait=_mx(tV, w0), inc="G")
        yield
        tSq = P.add("gp", lambda e, b=b: e.tensor_tensor(
            out=b[1], in0=b[0], in1=b[0], op=Alu.mult), wait=[tY], inc="G")
        yield
        tWa = P.add("v", lambda e, b=b: e.tensor_scalar(
            out=b[2], in0=b[1], scalar1=256.0, scalar2=-0.5,
            op0=Alu.is_ge, op1=Alu.mult), wait=[tSq], inc="V")
        yield
        tWb = P.add("v", lambda e, b=b: e.tensor_scalar(
            out=b[3], in0=b[1], scalar1=1024.0, scalar2=-0.25,
            op0=Alu.is_ge, op1=Alu.mult), wait=[tSq], inc="V")
        yield
        tW2 = P.add("v", lambda e, b=b: e.scalar_tensor_tensor(
            out=b[1], in0=b[2], scalar=1.0, in1=b[3],
            op0=Alu.add, op1=Alu.add), wait=[tWa, tWb], inc="V")
        yield
        tR = P.add("gp", lambda e, b=b: e.tensor_tensor(
            out=b[4], in0=b[0], in1=b[1], op=Alu.mult),
            wait=_mx(tY, tW2), inc="G")
        yield
        tIv = P.add("v", lambda e, b=b: e.reciprocal(
            out=b[3], in_=b[1]), wait=_mx(tW2), inc="V")
        yield
        tD = tR
        seq = [(qb[4], qb[5]), (qb[5], qb[1]), (qb[1], qb[5]),
               (qb[5], qb[1]), (qb[1], qb[5])]
        for (s_t, d_t), K in zip(seq, (16.0, 8.0, 4.0, 2.0, 1.0)):
            s_ap = s_t[:, 512 * set_i:512 * (set_i + 1)]
            d_ap = d_t[:, 512 * set_i:512 * (set_i + 1)]
            tD = P.add("v", lambda e, s=s_ap, d=d_ap, K=K: e.add_range_wrap(
                out=d, in_=s, shift=0.0, bound=K - 0.5, period=K),
                wait=[tD], inc="V")
            yield
        tN = P.add("gp", lambda e, b=b: e.tensor_tensor(
            out=b[0], in0=b[4], in1=b[5], op=Alu.subtract),
            wait=_mx(tR, tD), inc="G")
        yield
        tZ = P.add("gp", lambda e, b=b: e.tensor_tensor(
            out=b[4], in0=b[0], in1=b[3], op=Alu.mult),
            wait=_mx(tN, tIv), inc="G")
        yield
        tX = P.add("gp", lambda e, b=b, dst=dst_ap, s8b=s8b: e.tensor_tensor(
            out=dst.rearrange("p (nb s) -> p nb s", s=32),
            in0=b[4].rearrange("p (nb s) -> p nb s", s=32), in1=s8b,
            op=Alu.mult), wait=_mx(tZ, *dst_wait), inc="G")
        yield
        qset_prev[set_i] = tX
        out_tok[0] = tX
        yield

    wf_free = [None, None]
    wd_free = [None, None]
    xnf_free = [None, None]
    xdb_free = [None, None]
    tok_xd_store = []
    qstate = {"qi": 0, "ring": 0}

    def emit_wd_tile(wt):
        r = qstate["ring"] % 2
        qstate["ring"] += 1
        tL = P.add("sync", lambda e, r=r, wt=wt: e.dma_start(
            out=wf_ring[:, C * r:C * (r + 1)],
            in_=wffn_h[128 * wt:128 * (wt + 1), :]),
            wait=_mx(wf_free[r]), inc=f"SWF{r}", n=16)
        tq = []
        for hv in range(2):
            ot = [None]
            yield from emit_quant(
                wf_ring[:, C * r + 512 * hv:C * r + 512 * (hv + 1)],
                wd_ring[:, 512 * hv:512 * (hv + 1)],
                hv, [tL], [wd_free[hv]], ot)
            tq.append(ot[0])
            qstate["qi"] += 1
            tS = P.add("sync", lambda e, wt=wt, hv=hv: e.dma_start(
                out=wd_s[128 * wt:128 * (wt + 1), 512 * hv:512 * (hv + 1)],
                in_=wd_ring[:, 512 * hv:512 * (hv + 1)]),
                wait=_mx(ot[0]), inc="SWDST0", n=16)
            wd_free[hv] = tS
        wf_free[r] = _mx(*tq)

    def emit_quant_tile(t):
        r = qstate["ring"] % 2
        qstate["ring"] += 1
        cc_ = {k: stats[k][:, t:t + 1] for k in ("rstd", "nmr")}
        tL = P.add("sync", lambda e, r=r, t=t: e.dma_start(
            out=wf_ring[:, C * r:C * (r + 1)],
            in_=x_h[128 * t:128 * (t + 1), :]),
            wait=_mx(wf_free[r]), inc=f"SWF{r}", n=16)
        tap = P.add("a", lambda e, r=r, c=cc_: e.activation(
            out=xnf_ring[:, C * r:C * (r + 1)], in_=wf_ring[:, C * r:C * (r + 1)],
            func=Act.Identity, scale=c["rstd"], bias=c["nmr"]),
            wait=_mx(tL, xnf_free[r], tok_stats[t]), inc="A")
        wf_free[r] = tap
        tq = []
        for hv in range(2):
            ot = [None]
            yield from emit_quant(
                xnf_ring[:, C * r + 512 * hv:C * r + 512 * (hv + 1)],
                xdb_ring[:, C * r + 512 * hv:C * r + 512 * (hv + 1)],
                hv, [tap], [xdb_free[r]], ot)
            tq.append(ot[0])
            qstate["qi"] += 1
        xnf_free[r] = _mx(*tq)
        ts2 = P.add("sync", lambda e, r=r, t=t: e.dma_start(
            out=xd_s[128 * t:128 * (t + 1), :],
            in_=xdb_ring[:, C * r:C * (r + 1)]), wait=_mx(*tq),
            inc=f"SXDST{r}", n=16)
        tok_xd_store.append(ts2)
        xdb_free[r] = ts2

    if _MAX_PHASE >= 2:
        x_free = [None] * 3
        xnb_free = [None, None]
        tok_xn_store = []
        tok_stats = {}
        for t in range(16):
            r4, r2 = t % 3, t % 2
            cc_ = {k: stats[k][:, t:t + 1] for k in stats}
            tL = P.add("sync", lambda e, r4=r4, t=t: e.dma_start(
                out=x_ring[:, C * r4:C * (r4 + 1)],
                in_=x_h[128 * t:128 * (t + 1), :]),
                wait=_mx(x_free[r4]), inc=f"SX{r4}", n=16)
            xt = x_ring[:, C * r4:C * (r4 + 1)]
            tS1 = P.add("v", lambda e, xt=xt, c=cc_: e.tensor_reduce(
                out=c["su"], in_=xt, axis=AXX, op=Alu.add), wait=[tL], inc="V")
            tS2 = P.add("v", lambda e, xt=xt, c=cc_, r2=r2: e.tensor_tensor_reduce(
                out=xnb_ring[:, C * r2:C * (r2 + 1)], in0=xt, in1=xt, scale=1.0,
                scalar=0.0, op0=Alu.mult, op1=Alu.add, accum_out=c["sq"]),
                wait=_mx(tL, xnb_free[r2]), inc="V")
            tc = P.add("v", lambda e, c=cc_: e.tensor_scalar(
                out=c["mean"], in0=c["su"], scalar1=1.0 / C, scalar2=0.0,
                op0=Alu.mult, op1=Alu.bypass), wait=[tS1], inc="V")
            tc = P.add("v", lambda e, c=cc_: e.tensor_scalar(
                out=c["ex2"], in0=c["sq"], scalar1=1.0 / C, scalar2=0.0,
                op0=Alu.mult, op1=Alu.bypass), wait=_mx(tS2, tc), inc="V")
            tc = P.add("v", lambda e, c=cc_: e.tensor_tensor(
                out=c["m2"], in0=c["mean"], in1=c["mean"], op=Alu.mult),
                wait=[tc], inc="V")
            tc = P.add("v", lambda e, c=cc_: e.tensor_tensor(
                out=c["var"], in0=c["ex2"], in1=c["m2"], op=Alu.subtract),
                wait=[tc], inc="V")
            tc = P.add("v", lambda e, c=cc_: e.tensor_scalar(
                out=c["ve"], in0=c["var"], scalar1=1e-5, scalar2=0.0,
                op0=Alu.add, op1=Alu.bypass), wait=[tc], inc="V")
            ta = P.add("a", lambda e, c=cc_: e.activation(
                out=c["sd"], in_=c["ve"], func=Act.Sqrt), wait=[tc], inc="A")
            tc = P.add("v", lambda e, c=cc_: e.reciprocal(
                out=c["r0"], in_=c["sd"]), wait=[ta], inc="V")
            tc = P.add("v", lambda e, c=cc_: e.tensor_tensor(
                out=c["t1"], in0=c["r0"], in1=c["r0"], op=Alu.mult),
                wait=[tc], inc="V")
            tc = P.add("v", lambda e, c=cc_: e.tensor_tensor(
                out=c["t2"], in0=c["t1"], in1=c["ve"], op=Alu.mult),
                wait=[tc], inc="V")
            tc = P.add("v", lambda e, c=cc_: e.tensor_scalar(
                out=c["t3"], in0=c["t2"], scalar1=-0.5, scalar2=1.5,
                op0=Alu.mult, op1=Alu.add), wait=[tc], inc="V")
            tc = P.add("v", lambda e, c=cc_: e.tensor_tensor(
                out=c["rstd"], in0=c["r0"], in1=c["t3"], op=Alu.mult),
                wait=[tc], inc="V")
            tc = P.add("v", lambda e, c=cc_: e.tensor_tensor(
                out=c["nmr"], in0=c["mean"], in1=c["rstd"], op=Alu.mult),
                wait=[tc], inc="V")
            tc = P.add("v", lambda e, c=cc_: e.tensor_scalar(
                out=c["nmr"], in0=c["nmr"], scalar1=-1.0, scalar2=0.0,
                op0=Alu.mult, op1=Alu.bypass), wait=[tc], inc="V")
            tok_stats[t] = tc
            tap = P.add("a", lambda e, xt=xt, c=cc_, r2=r2: e.activation(
                out=xnb_ring[:, C * r2:C * (r2 + 1)], in_=xt, func=Act.Identity,
                scale=c["rstd"], bias=c["nmr"]),
                wait=_mx(tc, xnb_free[r2], tS2), inc="A")
            x_free[r4] = tap
            ts_ = P.add("a", lambda e, t=t, r2=r2: e.dma_start(
                out=xn_s[128 * t:128 * (t + 1), :],
                in_=xnb_ring[:, C * r2:C * (r2 + 1)]), wait=[tap],
                inc=f"SXNST{t // 8}", n=16)
            tok_xn_store.append(ts_)
            xnb_free[r2] = ts_
        tok_xn_all = _mx(*tok_xn_store)
        tok_xn_half = [_mx(*tok_xn_store[:8]), _mx(*tok_xn_store[8:])]
        quant_work = ([("wd", wt) for wt in range(8)]
                      + [("q", t) for t in range(16)])
        qptr = [0]
        qgens = []
        qnewest = [None]

        def pump_quant(k):
            while k > 0:
                if ((len(qgens) < 2 and qptr[0] < len(quant_work))
                        and (qnewest[0] is None or qnewest[0][0] >= 21)):
                    kind, idx = quant_work[qptr[0]]
                    qptr[0] += 1
                    cnt = [0]
                    qnewest[0] = cnt
                    qgens.append(((emit_wd_tile if kind == "wd"
                                   else emit_quant_tile)(idx), cnt))
                if not qgens:
                    return
                gen, cnt = qgens.pop(0)
                try:
                    next(gen)
                    cnt[0] += 1
                    qgens.append((gen, cnt))
                    k -= 1
                except StopIteration:
                    pass

    if _MAX_PHASE >= 3:
        psA_guard = [None, None]
        psB_guard = [None] * 4
        tok_y_store = []
        tok_d_all = []
        tok_lat_mm_prev = None
        pring_free = [None, None]
        pm_free = [None, None]
        ystage_free = [None, None]

        txn = {}
        for cc in range(8):
            for th in range(2):
                txn[(cc, th)] = P.add("a", lambda e, cc=cc, th=th:
                                      e.dma_start_transpose(
                    out=xnT[:, T * cc + 1024 * th:T * cc + 1024 * (th + 1)],
                    in_=xn_s[1024 * th:1024 * (th + 1), 128 * cc:128 * (cc + 1)]),
                    wait=_mx(tok_xn_half[th],
                             txn[(cc, 0)] if th else None),
                    inc=f"SXNT{cc}", n=16)
        txn_all = _mx(*txn.values())

        wg_guard = [None, None]
        wcast_tok = {}

        def emit_wload(g):
            sl = g % 2
            wcast_tok[g] = {}
            for w in range(3):
                tL = None
                for cc in range(8):
                    tL = P.add("gp", lambda e, w=w, cc=cc, g=g, sl=sl:
                               e.dma_start(
                        out=wg_bf[w][:, 2048 * sl + 256 * cc:2048 * sl + 256 * (cc + 1)],
                        in_=W_H[w][128 * cc:128 * (cc + 1), 256 * g:256 * (g + 1)]),
                        wait=_mx(wg_guard[sl]), inc=f"SWC{sl}{w}", n=16)
                wcast_tok[g][w] = tL

        emit_wload(0)
        for g in range(_MAX_GROUPS):
            if g + 1 < _MAX_GROUPS:
                emit_wload(g + 1)
            wcast = wcast_tok[g]
            gsl = g % 2

            lat_evac = {}
            for w in range(3):
                for half in range(2):
                    ri = 2 * w + half
                    psX = psA if ri % 2 == 0 else psB
                    mm_tok = [None, None]
                    for t5h in range(2):
                        for cc in range(8):
                            off = 256 * cc + 128 * half

                            def mml(e, w=w, off=off, cc=cc, gsl=gsl, psX=psX,
                                    t5h=t5h):
                                lh = wg_bf[w][:, 2048 * gsl + off:2048 * gsl + off + 128]
                                e.ldweights(lh)
                                last = None
                                for t5 in (2 * t5h, 2 * t5h + 1):
                                    last = e.matmul(
                                        out=psX[:, 512 * t5:512 * (t5 + 1)],
                                        lhsT=lh,
                                        rhs=xnT[:, T * cc + 512 * t5:T * cc + 512 * (t5 + 1)],
                                        start=(cc == 0), stop=(cc == 7))
                                return last
                            guard = (psA_guard if ri % 2 == 0 else psB_guard)
                            mm_tok[t5h] = P.add(
                                "t", mml,
                                wait=_mx(wcast[w], txn[(cc, t5h)],
                                         *(guard if cc == 0 else [])), inc="TE")
                    ev_eng = "a"
                    for t5 in range(4):
                        def evl(e, w=w, half=half, t5=t5, psX=psX, ev=ev_eng):
                            dst = lat[w][:, T * half + 512 * t5:T * half + 512 * (t5 + 1)]
                            src = psX[:, 512 * t5:512 * (t5 + 1)]
                            return (e.copy(out=dst, in_=src) if ev == "a"
                                    else e.tensor_copy(out=dst, in_=src))
                        te = P.add(ev_eng, evl, wait=[mm_tok[t5 // 2]],
                                   inc="A" if ev_eng == "a" else "V")
                        lat_evac[(w, half, t5)] = te
                    if ri % 2 == 0:
                        psA_guard = [_mx(lat_evac[(w, half, 0)], lat_evac[(w, half, 1)]),
                                     _mx(lat_evac[(w, half, 2)], lat_evac[(w, half, 3)])]
                    else:
                        psB_guard = [_mx(lat_evac[(w, half, t5)]) for t5 in range(4)]
                    if g or ri:
                        pump_quant(10)
            tok_lat_mm_prev = P.tok("TE")
            wg_guard[gsl] = tok_lat_mm_prev
            lat_all = _mx(*lat_evac.values())
            if _P2_SUB < 2:
                continue

            for side in range(2):
                wup = wupq_s if side == 0 else wupk
                dst = qAB if side == 0 else kAB
                latw = lat[side]
                for t5 in range(4):
                    def mmu(e, latw=latw, wup=wup, t5=t5):
                        last = None
                        for j in range(4):
                            p0 = 64 * (j % 2)
                            pair = j // 2
                            rhs = latw[p0:p0 + 64,
                                       T * (j // 2) + 512 * t5:T * (j // 2) + 512 * (t5 + 1)]
                            wl = wup[p0:p0 + 64, :]
                            mrow = 64 * (j % 2)
                            bce = 512 * pair
                            brot = 512 * (2 + pair)
                            e.ldweights(wl[:, 0:32], tile_position=(p0, mrow))
                            e.matmul(out=psB[mrow:mrow + 32, bce:bce + 512],
                                     lhsT=wl[:, 0:32], rhs=rhs, start=True,
                                     stop=True, tile_position=(p0, mrow))
                            e.ldweights(wl[:, 32:64], tile_position=(p0, mrow + 32))
                            e.matmul(out=psB[mrow + 32:mrow + 64, bce:bce + 512],
                                     lhsT=wl[:, 32:64], rhs=rhs, start=True,
                                     stop=True, tile_position=(p0, mrow + 32))
                            e.ldweights(wl[:, 64:96], tile_position=(p0, mrow + 32))
                            last = e.matmul(out=psB[mrow + 32:mrow + 64, brot:brot + 512],
                                            lhsT=wl[:, 64:96], rhs=rhs, start=True,
                                            stop=True, tile_position=(p0, mrow + 32))
                        return last
                    wl_t = [lat_evac[(side, hh, t5)] for hh in range(2)]
                    mm_tok = P.add("t", mmu,
                                   wait=_mx(*(wl_t + psB_guard + [tok_up, tok_ps_init])),
                                   inc="TE")
                    if g:
                        pump_quant(4)
                    for pair in range(2):
                        bce = 512 * pair
                        brot = 512 * (2 + pair)
                        tm1 = P.add("v", lambda e, bce=bce, t5=t5: e.tensor_tensor(
                            out=rp1[:], in0=psB[:, bce:bce + 512],
                            in1=c_cos4[:, 512 * t5:512 * (t5 + 1)], op=Alu.mult),
                            wait=_mx(mm_tok, t_consts), inc="V")
                        tm2 = P.add("v", lambda e, brot=brot, t5=t5: e.tensor_tensor(
                            out=rp2[:], in0=psB[:, brot:brot + 512],
                            in1=c_sin4[:, 512 * t5:512 * (t5 + 1)], op=Alu.mult),
                            wait=_mx(mm_tok, t_consts), inc="V")
                        tro = P.add("v", lambda e, dst=dst, pair=pair, t5=t5:
                                    e.tensor_tensor(
                            out=dst[pair][:, 512 * t5:512 * (t5 + 1)], in0=rp1[:],
                            in1=rp2[:], op=Alu.add), wait=[tm1, tm2], inc="V")
                        psB_guard[pair] = tm1
                        psB_guard[2 + pair] = _mx(tm2, tro)
            tok_qk = P.tok("V")
            if _P2_SUB < 3:
                continue

            vz = []
            for j in range(4):
                h = 4 * g + j
                tz = P.add("v", lambda e, j=j: e.memset(v_ext[j][:], 0.0),
                           wait=_mx(tok_lat_mm_prev if g else None), inc="V")
                to = P.add("v", lambda e, j=j, h=h: e.memset(
                    v_ext[j][:].rearrange("p (a b) -> p a b", b=80)[:, :, 64 + h:65 + h],
                    1.0), wait=[tz], inc="V")
                vz.append(to)
            for j in range(4):
                for r2 in range(2):
                    def mmv(e, j=j, r2=r2):
                        last = None
                        for k in range(8):
                            tc2 = 8 * r2 + k
                            lhs = lat[2][64 * (j % 2):64 * (j % 2) + 64,
                                         T * (j // 2) + 128 * tc2:T * (j // 2) + 128 * (tc2 + 1)]
                            p0 = 64 * (j % 2)
                            e.ldweights(lhs, tile_position=(p0, 0))
                            last = e.matmul(
                                out=psB[:, 1536 + 64 * k:1536 + 64 * (k + 1)],
                                lhsT=lhs, rhs=wupv[p0:p0 + 64, :], start=True, stop=True,
                                tile_position=(p0, 0))
                        return last
                    mm_tok = P.add("t", mmv,
                                   wait=_mx(lat_all, tok_up, psB_guard[3]),
                                   inc="TE")
                    if g:
                        pump_quant(3)
                    tev = P.add("v", lambda e, j=j, r2=r2: e.tensor_copy(
                        out=v_ext[j][:, 80 * 8 * r2:80 * 8 * (r2 + 1)].rearrange(
                            "p (a b) -> p a b", b=80)[:, :, 0:64],
                        in_=psB[:, 1536:2048].rearrange("p (a b) -> p a b", b=64)),
                        wait=_mx(mm_tok, vz[j]), inc="V")
                    psB_guard[3] = tev
            tok_v = P.tok("V")
            if _P2_SUB < 4:
                continue

            entries = []
            for pa in range(2):
                for qt in range(4):
                    nkc = 4 * qt + 4
                    order = (list(range(4 * qt, 4 * qt + 4))
                             + list(range(0, 4 * qt)))
                    for oi, kc in enumerate(order):
                        entries.append(dict(
                            pa=pa, qt=qt, kc=kc, u=4 * pa + qt,
                            diag=(kc >= 4 * qt), first=(oi == 0),
                            last=(oi == nkc - 1)))
            NE = len(entries)

            def emit_sc(gi):
                en = entries[gi]
                s = gi % 2
                en["s"] = s
                pa, qt, kc = en["pa"], en["qt"], en["kc"]
                def mmsc(e, pa=pa, qt=qt, kc=kc, s=s):
                    l0 = kAB[pa][0:64, 128 * kc:128 * (kc + 1)]
                    l1 = kAB[pa][64:128, 128 * kc:128 * (kc + 1)]
                    e.ldweights(l0, tile_position=(0, 0))
                    e.ldweights(l1, tile_position=(64, 0))
                    e.matmul(out=psA[:, 1024 * s:1024 * s + 512], lhsT=l0,
                             rhs=qAB[pa][0:64, 512 * qt:512 * (qt + 1)],
                             start=True, stop=True, tile_position=(0, 0))
                    return e.matmul(
                        out=psA[:, 1024 * s + 512:1024 * (s + 1)], lhsT=l1,
                        rhs=qAB[pa][64:128, 512 * qt:512 * (qt + 1)],
                        start=True, stop=True, tile_position=(64, 0))
                tsc = P.add("t", mmsc, wait=_mx(tok_qk, psA_guard[s]), inc="TE")
                if en["diag"]:
                    pm = gi % 2
                    m = kc - 4 * qt
                    en["a0"] = 128 * m
                    tE = P.add("a", lambda e, s=s, pm=pm: e.activation(
                        out=p_m[:, 1024 * pm:1024 * (pm + 1)],
                        in_=psA[:, 1024 * s:1024 * (s + 1)], func=Act.Exp),
                        wait=_mx(tsc, pm_free[pm]), inc="A")
                    psA_guard[s] = tE
                    tMs = []
                    for jj in range(2):
                        o0 = 1024 * pm + 512 * jj + 128 * m
                        tMs.append(P.add("v", lambda e, o0=o0: e.tensor_tensor(
                            out=p_m[:, o0:o0 + 128],
                            in0=p_m[:, o0:o0 + 128],
                            in1=c_mask[:, 0:128], op=Alu.mult),
                            wait=_mx(tE, t_consts), inc="V"))
                    en["tok"], en["pm"] = _mx(*tMs), pm
                else:
                    en["a0"] = 0
                    tE = P.add("a", lambda e, s=s: e.activation(
                        out=p_ring[:, 1024 * s:1024 * (s + 1)],
                        in_=psA[:, 1024 * s:1024 * (s + 1)], func=Act.Exp),
                        wait=_mx(tsc, pring_free[s]), inc="A")
                    psA_guard[s] = tE
                    en["tok"], en["pm"] = tE, None

            def emit_pv(gi):
                en = entries[gi]
                s2, pm2 = en["s"], en["pm"]
                pa, qt, kc, u = en["pa"], en["qt"], en["kc"], en["u"]
                first, last_kc = en["first"], en["last"]
                a0 = en["a0"]
                def mmpv(e, kc=kc, s2=s2, pm2=pm2, pa=pa, a0=a0,
                         first=first, last_kc=last_kc):
                    src_t = p_m if pm2 is not None else p_ring
                    off = 1024 * (pm2 if pm2 is not None else s2)
                    last = None
                    for combo in (((0, 0), (1, 1)), ((1, 0), (0, 1))):
                        lhs = []
                        for jj, hh in combo:
                            j = 2 * pa + jj
                            p0 = 64 * hh
                            lh = v_ext[j][p0:p0 + 64, 80 * kc:80 * (kc + 1)]
                            e.ldweights(lh, tile_position=(p0, 0))
                            lhs.append((jj, hh, p0, lh))
                        for jj, hh, p0, lh in lhs:
                            src = src_t[p0:p0 + 64,
                                        off + 512 * jj + a0:off + 512 * (jj + 1)]
                            bk0 = 512 * (2 * jj + hh)
                            last = e.matmul(
                                out=psB[0:80, bk0 + a0:bk0 + 512],
                                lhsT=lh, rhs=src, start=first, stop=last_kc,
                                tile_position=(p0, 0))
                    return last
                wpv = _mx(en["tok"], tok_v,
                          _mx(*psB_guard) if first else None)
                tpv = P.add("t", mmpv, wait=wpv, inc="TE")
                if pm2 is not None:
                    pm_free[pm2] = tpv
                else:
                    pring_free[s2] = tpv
                if last_kc:
                    for jj in range(2):
                        h = 4 * g + 2 * pa + jj
                        bkA, bkB = 2 * jj, 2 * jj + 1
                        r2 = (2 * u + jj) % 2
                        tyc = P.add("a", lambda e, bkA=bkA, r2=r2: e.copy(
                            out=ystage[:, 512 * r2:512 * (r2 + 1)],
                            in_=psB[0:80, 512 * bkA:512 * (bkA + 1)]),
                            wait=_mx(tpv, ystage_free[r2]), inc="A")
                        tye = P.add("v", lambda e, bkB=bkB, r2=r2:
                                    e.tensor_tensor(
                            out=ystage[:, 512 * r2:512 * (r2 + 1)],
                            in0=ystage[:, 512 * r2:512 * (r2 + 1)],
                            in1=psB[0:80, 512 * bkB:512 * (bkB + 1)],
                            op=Alu.add),
                            wait=[tyc], inc="V")
                        tde = P.add("v", lambda e, qt=qt, r2=r2:
                                    e.tensor_tensor(
                            out=dbuf[64:80, 512 * qt:512 * (qt + 1)],
                            in0=dbuf[64:80, 512 * qt:512 * (qt + 1)],
                            in1=ystage[64:80, 512 * r2:512 * (r2 + 1)],
                            op=Alu.add),
                            wait=_mx(tye, tok_dbuf_init), inc="V")
                        tok_d_all.append(tde)
                        tst = P.add("sync", lambda e, qt=qt, h=h, r2=r2:
                                    e.dma_start(
                            out=y_s[64 * h:64 * (h + 1), 512 * qt:512 * (qt + 1)],
                            in_=ystage[0:64, 512 * r2:512 * (r2 + 1)]),
                            wait=[tye], inc=f"SYS{r2}", n=16)
                        tok_y_store.append(tst)
                        ystage_free[r2] = _mx(tst, tde)
                        psB_guard[bkA] = _mx(tyc)
                        psB_guard[bkB] = _mx(tye)

            for gi in range(NE + 2):
                if gi >= 2:
                    emit_pv(gi - 2)
                if gi < NE:
                    emit_sc(gi)
                if not (g == 0 and gi < 12):
                    pump_quant(2)


        pump_quant(10 ** 9)
        tok_y_all = _mx(*tok_y_store)
        tok_xd_all = _mx(*tok_xd_store)
        tok_wd_done = _mx(P.tok("SWDST0"))

    if _MAX_PHASE >= 4:
        trec = P.add("v", lambda e: e.reciprocal(out=dbuf[64:80, :],
                                                 in_=dbuf[64:80, :]),
                     wait=_mx(*tok_d_all), inc="V")

        for cm in reversed(cmsB):
            cm.__exit__(None, None, None)
        cmsB.clear()
        wdT = sb("wdT", (128, 8 * C), BF16)
        xdT = sb("xdT", (128, 2 * 8 * 512), BF16)
        yn = sb("yn", (128, 8 * T), BF16)
        yld = sb("yld", (128, 2 * T), BF16)
        wc_bf = sb("wc_bf", (128, 8 * C), BF16)
        ffn_stage = sb("ffn_stage", (128, 2 * 512), BF16)
        ffn_ld = sb("ffn_ld", (128, 2 * 512), BF16)
        tmpc = sb("tmpc", (128, 2 * 512), F32)
        out_sb = sb("out_sb", (128, 4 * C), F32)
        x4_extra = sb("x4_extra", (128, C), F32)
        c_ph = sb("c_ph", (80, 1024), F32)

        barrier = _mx(*[(s_, c_) for s_, c_ in P.count.items()])
        t_ph = P.add("sync", lambda e: e.dma_start(out=c_ph[:], in_=ph_h),
                     wait=_mx(barrier), inc="S_CONST", n=16)

        twdT = []
        for cc in range(8):
            twdT.append(P.add("a",
                              lambda e, cc=cc: e.dma_start_transpose(
                out=wdT[:, C * cc:C * (cc + 1)],
                in_=wd_s[:, 128 * cc:128 * (cc + 1)]),
                wait=_mx(tok_wd_done, tok_xd_all), inc="SWDT", n=16))
        xdT_free = [None, None]
        ffn_stage_free = [None, None]
        ffn_store = {}
        psA_free2 = [barrier, barrier]
        for t5 in range(4):
            r = t5 % 2
            txd = []
            for cc in range(8):
                txd.append(P.add("a",
                                 lambda e, cc=cc, t5=t5, r=r:
                                 e.dma_start_transpose(
                    out=xdT[:, 512 * (8 * r + cc):512 * (8 * r + cc + 1)],
                    in_=xd_s[512 * t5:512 * (t5 + 1), 128 * cc:128 * (cc + 1)]),
                    wait=_mx(tok_xd_all, xdT_free[r]), inc=f"SXDT{r}", n=16))
            last_mm = None
            for co in range(8):
                b = co % 2

                def mmf(e, co=co, r=r):
                    last = None
                    for cc in range(8):
                        e.ldweights(wdT[:, C * cc + 128 * co:C * cc + 128 * (co + 1)])
                        last = e.matmul(
                            out=psA[:, 512 * (co % 2):512 * (co % 2 + 1)],
                            lhsT=wdT[:, C * cc + 128 * co:C * cc + 128 * (co + 1)],
                            rhs=xdT[:, 512 * (8 * r + cc):512 * (8 * r + cc + 1)],
                            start=(cc == 0), stop=(cc == 7))
                    return last
                tmm = P.add("t", mmf, wait=_mx(*(txd + twdT + [psA_free2[b]])),
                            inc="TE")
                tev = P.add("a", lambda e, co=co, b=b: e.activation(
                    out=ffn_stage[:, 512 * b:512 * (b + 1)],
                    in_=psA[:, 512 * b:512 * (b + 1)], func=Act.Identity,
                    scale=1.0, bias=c_bias[:, co:co + 1]),
                    wait=_mx(tmm, ffn_stage_free[b]), inc="A")
                psA_free2[b] = tev
                tst = P.add("sync", lambda e, co=co, t5=t5, b=b: e.dma_start(
                    out=ffn_s[128 * co:128 * (co + 1), 512 * t5:512 * (t5 + 1)],
                    in_=ffn_stage[:, 512 * b:512 * (b + 1)]),
                    wait=[tev], inc=f"SFST{b}", n=16)
                ffn_stage_free[b] = tst
                ffn_store[(co, t5)] = tst
                last_mm = tmm
            xdT_free[r] = last_mm

        yld_free = [None, None]
        psB_free3 = [barrier] * 4
        tyn = {}
        for cc in range(8):
            r = cc % 2
            tly = P.add(("a", "sync")[r], lambda e, cc=cc, r=r: e.dma_start(
                out=yld[:, T * r:T * (r + 1)], in_=y_s[128 * cc:128 * (cc + 1), :]),
                wait=_mx(tok_y_all, yld_free[r], barrier), inc=f"SYLD{r}", n=16)
            for t5 in range(4):
                bb = t5 % 4

                def mmb(e, cc=cc, t5=t5, bb=bb):
                    return e.matmul(out=psB[:, 512 * bb:512 * (bb + 1)],
                                    lhsT=c_ph[64:80, 128 * cc:128 * (cc + 1)],
                                    rhs=dbuf[64:80, 512 * t5:512 * (t5 + 1)],
                                    start=True, stop=True)
                tbm = P.add("t", mmb, wait=_mx(trec, t_ph, psB_free3[bb],
                                               barrier), inc="TE")
                tn = P.add("v", lambda e, cc=cc, t5=t5, r=r, bb=bb: e.tensor_tensor(
                    out=yn[:, T * cc + 512 * t5:T * cc + 512 * (t5 + 1)],
                    in0=yld[:, T * r + 512 * t5:T * r + 512 * (t5 + 1)],
                    in1=psB[:, 512 * bb:512 * (bb + 1)], op=Alu.mult),
                    wait=_mx(tly, tbm), inc="V")
                psB_free3[bb] = tn
                tyn[(cc, t5)] = tn
            yld_free[r] = tyn[(cc, 3)]
        twc = {}
        for cc in range(8):
            twc[cc] = P.add("gp", lambda e, cc=cc: e.dma_start(
                out=wc_bf[:, C * cc:C * (cc + 1)],
                in_=wc_h[128 * cc:128 * (cc + 1), :]),
                wait=_mx(barrier), inc="SWC0", n=16)
        twc_all = _mx(*twc.values())
        tyn_all = _mx(*tyn.values())

        x2_free = [barrier] * 4
        xtile = lambda tt: x4_extra[:] if tt == 3 else x_ring[:, C * tt:C * (tt + 1)]
        out_free = [None] * 4
        ffn_ld_free = [None, None]
        tmp_free = [None, None]
        psA_free3 = list(psA_free2)
        psB_free4 = [psB_free3[0], psB_free3[1]]
        for t5 in range(4):
            tx = {}
            for tt in range(4):
                t128 = 4 * t5 + tt
                tx[tt] = P.add("sync", lambda e, t128=t128, tt=tt: e.dma_start(
                    out=xtile(tt),
                    in_=x_h[128 * t128:128 * (t128 + 1), :]),
                    wait=_mx(x2_free[tt], barrier), inc=f"SX{tt}", n=16)
            tadds = {tt: [] for tt in range(4)}
            for co in range(8):
                b = co % 2

                def mmw(e, co=co, t5=t5, b=b):
                    last = None
                    for cc in range(8):
                        e.ldweights(wc_bf[:, C * cc + 128 * co:C * cc + 128 * (co + 1)])
                        last = e.matmul(
                            out=psA[:, 512 * b:512 * (b + 1)],
                            lhsT=wc_bf[:, C * cc + 128 * co:C * cc + 128 * (co + 1)],
                            rhs=yn[:, T * cc + 512 * t5:T * cc + 512 * (t5 + 1)],
                            start=(cc == 0), stop=(cc == 7))
                    return last
                tmm = P.add("t", mmw,
                            wait=_mx(*([tyn[(cc, t5)] for cc in range(8)]
                                       + [twc_all, psA_free3[b]])),
                            inc="TE")
                tlf = P.add("gp", lambda e, co=co, t5=t5, b=b: e.dma_start(
                    out=ffn_ld[:, 512 * b:512 * (b + 1)],
                    in_=ffn_s[128 * co:128 * (co + 1), 512 * t5:512 * (t5 + 1)]),
                    wait=_mx(ffn_store[(co, t5)], ffn_ld_free[b]), inc=f"SFLD{b}", n=16)
                tcb = P.add("v", lambda e, b=b: e.tensor_tensor(
                    out=tmpc[:, 512 * b:512 * (b + 1)],
                    in0=psA[:, 512 * b:512 * (b + 1)],
                    in1=ffn_ld[:, 512 * b:512 * (b + 1)], op=Alu.add),
                    wait=_mx(tmm, tlf, tmp_free[b]), inc="V")
                psA_free3[b] = tcb
                ffn_ld_free[b] = tcb
                last_add = None
                for tt in range(4):
                    pb = (co * 4 + tt) % 2

                    def mtr(e, b=b, tt=tt, pb=pb):
                        return e.transpose(
                            out=psB[:, 512 * pb:512 * pb + 128],
                            in_=tmpc[:, 512 * b + 128 * tt:512 * b + 128 * (tt + 1)],
                            identity=c_idn[:])
                    ttr = P.add("t", mtr, wait=_mx(tcb, psB_free4[pb], t_consts),
                                inc="TE")
                    tad = P.add("v", lambda e, co=co, tt=tt, pb=pb: e.tensor_tensor(
                        out=out_sb[:, C * tt + 128 * co:C * tt + 128 * (co + 1)],
                        in0=psB[:, 512 * pb:512 * pb + 128],
                        in1=xtile(tt)[:, 128 * co:128 * (co + 1)],
                        op=Alu.add), wait=_mx(ttr, tx[tt], out_free[tt]), inc="V")
                    psB_free4[pb] = tad
                    tadds[tt].append(tad)
                    last_add = tad
                tmp_free[b] = last_add
            for tt in range(4):
                t128 = 4 * t5 + tt
                tst = P.add("sync", lambda e, t128=t128, tt=tt: e.dma_start(
                    out=out_h[128 * t128:128 * (t128 + 1), :],
                    in_=out_sb[:, C * tt:C * (tt + 1)]),
                    wait=_mx(*tadds[tt]), inc=f"SOUT{tt}", n=16)
                out_free[tt] = tst
                x2_free[tt] = _mx(*tadds[tt])

    P.add("sync", lambda e: None,
          wait=[(f"SOUT{tt}", P.count.get(f"SOUT{tt}", 0)) for tt in range(4)])

    used = set(P.count)
    for strm in P.streams.values():
        for waits_, _, _, _ in strm:
            used.update(s for s, _ in waits_)
    sems = {s: ent(nc.semaphore(s)) for s in sorted(used)}
    block = ent(nc.Block())

    @block.sync
    def _(e):
        P.realize("sync", e, sems)

    @block.gpsimd
    def _(e):
        P.realize("gp", e, sems)

    @block.vector
    def _(e):
        P.realize("v", e, sems)

    @block.scalar
    def _(e):
        P.realize("a", e, sems)

    @block.tensor
    def _(e):
        P.realize("t", e, sems)

    for cm in reversed(cmsB):
        cm.__exit__(None, None, None)
    for cm in reversed(cms):
        cm.__exit__(None, None, None)
    return nc


def kernel(**inputs):
    in_maps = _layout_inputs(inputs)
    nc = build(taps=_DEBUG_TAPS)
    mybir.codegen_inst_isa_subclasses(nc)
    res = run_bass_kernel_spmd(nc, in_maps, core_ids=list(range(NCORES)))
    out = np.stack([np.asarray(r["out"]) for r in res.results], axis=0)
    if _DEBUG_TAPS:
        kernel._taps = [{k: np.asarray(r[k]) for k in _DEBUG_TAPS}
                        for r in res.results]
    return out


# version 34
# speedup vs baseline: 1.0053x; 1.0053x over previous
import numpy as np
import ml_dtypes
import concourse.bass as bass
import concourse.mybir as mybir
from concourse.bass_utils import run_bass_kernel_spmd

F32 = mybir.dt.float32
BF16 = mybir.dt.bfloat16
Alu = mybir.AluOpType
Act = mybir.ActivationFunctionType
AXX = mybir.AxisListType.X

T, C = 2048, 1024
NCORES = 8

_DEBUG_TAPS = ()
_DEFER_SCHED = {-1: [("q", 0), ("q", 1)],
                0: [("wd", wt) for wt in range(6)],
                1: [("wd", 6), ("wd", 7)] + [("q", t) for t in range(2, 6)],
                2: [("q", t) for t in range(6, 12)],
                3: [("q", t) for t in range(12, 16)]}
_MAX_PHASE = 99
_MAX_GROUPS = 4
_P2_SUB = 9


def _swap_perm():
    p = np.arange(32)
    p[0::2] += 1
    p[1::2] -= 1
    return p


def _host_consts():
    bf = ml_dtypes.bfloat16
    t = np.arange(T, dtype=np.float64)
    inv = 1.0 / (10000.0 ** (np.arange(0, 32, 2, dtype=np.float64) / 32))
    emb = t[:, None] * np.concatenate([inv, inv])[None, :]
    cos = np.cos(emb).T
    sin_fold = np.sin(emb).T.copy()
    sin_fold[0::2, :] *= -1.0
    ones32 = np.ones((32, T))
    zer32 = np.zeros((32, T))
    cos4 = np.concatenate([ones32, cos, ones32, cos], axis=0).astype(bf)
    sin4 = np.concatenate([zer32, sin_fold, zer32, sin_fold], axis=0).astype(bf)
    j = np.arange(128)[:, None]
    i = np.arange(512)[None, :]
    mask4 = np.concatenate(
        [np.tile((i >= j + 128 * m).astype(np.float32), (1, 2))
         for m in range(4)], axis=1).astype(bf)
    ph = np.zeros((80, 8 * 128), dtype=np.float32)
    for p in range(8):
        ph[64 + 2 * p, 128 * p:128 * p + 64] = 1.0
        ph[64 + 2 * p + 1, 128 * p + 64:128 * p + 128] = 1.0
    idn = np.eye(128, dtype=np.float32)
    return {"cos4": cos4, "sin4": sin4, "mask4": mask4,
            "pairhot": ph, "idn32": idn}


def _layout_inputs(inputs):
    consts = _host_consts()
    sp = _swap_perm()
    Wq_up = np.concatenate(
        [np.asarray(inputs["L0_attn_Wq_up_c"]), np.asarray(inputs["L0_attn_Wq_up_e"]),
         np.asarray(inputs["L0_attn_Wq_up_e"])[:, sp]], axis=1).astype(np.float32)
    Wk_up = np.concatenate(
        [np.asarray(inputs["L0_attn_Wk_up_c"]), np.asarray(inputs["L0_attn_Wk_up_e"]),
         np.asarray(inputs["L0_attn_Wk_up_e"])[:, sp]], axis=1).astype(np.float32)
    biascol = np.ascontiguousarray(
        np.asarray(inputs["L0_ffn_bias"], dtype=np.float32).reshape(8, 128).T)
    x = np.ascontiguousarray(np.asarray(inputs["L0_attn_x"], dtype=np.float32))
    shared = {
        "wq_d": np.ascontiguousarray(np.asarray(inputs["L0_attn_Wq_down"], dtype=np.float32)),
        "wk_d": np.ascontiguousarray(np.asarray(inputs["L0_attn_Wk_down"], dtype=np.float32)),
        "wv_d": np.ascontiguousarray(np.asarray(inputs["L0_attn_Wv_down"], dtype=np.float32)),
        "wq_up": Wq_up, "wk_up": Wk_up,
        "wv_up": np.ascontiguousarray(np.asarray(inputs["L0_attn_Wv_up"], dtype=np.float32)),
        "wc": np.ascontiguousarray(np.asarray(inputs["L0_attn_Wc"], dtype=np.float32)),
        "wffn": np.ascontiguousarray(np.asarray(inputs["L0_ffn_weight"], dtype=np.float32)),
        "biascol": biascol,
        **consts,
    }
    return [dict(shared, x=x[b]) for b in range(NCORES)]


ENGS = ("sync", "gp", "v", "a", "t")


class Plan:
    def __init__(self):
        self.streams = {e: [] for e in ENGS}
        self.count = {}

    def add(self, eng, emit, wait=(), inc=None, n=1):
        tok = None
        if inc is not None:
            self.count[inc] = self.count.get(inc, 0) + n
            tok = (inc, self.count[inc])
        waits = tuple(w for w in wait if w is not None)
        self.streams[eng].append((waits, emit, inc, n))
        return tok

    def tok(self, sem):
        return (sem, self.count.get(sem, 0))

    def realize(self, eng_name, eng, sems):
        hi = {}
        for waits, emit, inc, n in self.streams[eng_name]:
            for s, v in waits:
                if v > hi.get(s, 0):
                    eng.wait_ge(sems[s], v)
                    hi[s] = v
            inst = emit(eng)
            if inc is not None:
                if inst is not None:
                    inst.then_inc(sems[inc], n)
                else:
                    eng.sem_inc(sems[inc], n)


def _mx(*toks):
    best = {}
    for t_ in toks:
        if t_ is None:
            continue
        if isinstance(t_, list):
            for s, v in t_:
                if v > best.get(s, 0):
                    best[s] = v
        else:
            s, v = t_
            if v > best.get(s, 0):
                best[s] = v
    return [(s, v) for s, v in best.items()]


_LAST_PLAN = None


def build(taps=()):
    global _LAST_PLAN
    nc = bass.Bass("TRN2", target_bir_lowering=False)
    P = Plan()
    _LAST_PLAN = P

    def dram(name, shape, dt=F32, kind="ExternalInput"):
        return nc.dram_tensor(name, list(shape), dt, kind=kind).ap()

    x_h = dram("x", (T, C))
    W_H = (dram("wq_d", (C, C)), dram("wk_d", (C, C)), dram("wv_d", (C, C)))
    wqu_h = dram("wq_up", (64, 96))
    wku_h = dram("wk_up", (64, 96))
    wvu_h = dram("wv_up", (64, 64))
    wc_h = dram("wc", (C, C))
    wffn_h = dram("wffn", (C, C))
    bias_h = dram("biascol", (128, 8))
    cos4_h = dram("cos4", (128, T), BF16)
    sin4_h = dram("sin4", (128, T), BF16)
    mask4_h = dram("mask4", (128, 4096), BF16)
    ph_h = dram("pairhot", (80, 1024))
    idn_h = dram("idn32", (128, 128))
    out_h = dram("out", (T, C), kind="ExternalOutput")
    sk = lambda n: "ExternalOutput" if n in taps else "Internal"
    xn_s = dram("xn_s", (T, C), BF16, kind=sk("xn_s"))
    xd_s = dram("xd_s", (T, C), BF16, kind=sk("xd_s"))
    wd_s = dram("wd_s", (C, C), BF16, kind=sk("wd_s"))
    y_s = dram("y_s", (C, T), BF16, kind=sk("y_s"))
    ffn_s = dram("ffn_s", (C, T), BF16, kind=sk("ffn_s"))

    cms = []

    def ent(cm):
        cms.append(cm)
        return cm.__enter__()

    def sb(name, shape, dt=F32):
        return ent(nc.sbuf_tensor(name, list(shape), dt))

    psA = ent(nc.psum_tensor("psA", [128, 2048], F32))
    psB = ent(nc.psum_tensor("psB", [128, 2048], F32))
    c_cos4 = sb("c_cos4", (128, T), BF16)
    c_sin4 = sb("c_sin4", (128, T), BF16)
    c_mask = sb("c_mask", (128, 4096), BF16)
    c_idn = sb("c_idn", (128, 128), F32)
    c_bias = sb("c_bias", (128, 8), F32)
    x_ring = sb("x_ring", (128, 3 * C), F32)
    dbuf = sb("dbuf", (80, T), F32)
    p_ring = sb("p_ring", (128, 2048), BF16)
    p_m = sb("p_m", (128, 2048), BF16)

    cmsB = []

    def sbB(name, shape, dt=F32):
        cm = nc.sbuf_tensor(name, list(shape), dt)
        cmsB.append(cm)
        return cm.__enter__()

    wf_ring = sbB("wf_ring", (128, 2 * C), F32)
    wd_ring = sbB("wd_ring", (128, C), BF16)
    xnf_ring = sbB("xnf_ring", (128, 2 * C), F32)
    xnb_ring = sbB("xnb_ring", (128, 2 * C), BF16)
    xdb_ring = sbB("xdb_ring", (128, 2 * C), BF16)
    qb = [sbB(f"qb{i}", (128, 2 * 512), F32) for i in range(6)]
    bm = [sbB(f"bm{i}", (128, 2 * 16), F32) for i in range(3)]
    stats = {k: sbB("st_" + k, (128, 16), F32)
             for k in ("su", "sq", "mean", "ex2", "m2", "var", "ve", "sd",
                       "r0", "t1", "t2", "t3", "rstd", "nmr")}
    wupq = sbB("wupq", (128, 96), BF16)
    wupq_s = sbB("wupq_s", (128, 96), BF16)
    wupk = sbB("wupk", (128, 96), BF16)
    wupv = sbB("wupv", (128, 64), BF16)
    wg_bf = [sbB(f"wg{w}", (128, 2 * 2048), BF16) for w in range(3)]
    xnT = sbB("xnT", (128, 8 * T), BF16)
    lat = [sbB(f"lat{w}", (128, 2 * T), BF16) for w in range(3)]
    qAB = [sbB(f"qAB{p}", (128, T), BF16) for p in range(2)]
    kAB = [sbB(f"kAB{p}", (128, T), BF16) for p in range(2)]
    v_ext = [sbB(f"vext{j}", (128, 16 * 80), BF16) for j in range(4)]
    rp1 = sbB("rp1", (128, 512), BF16)
    rp2 = sbB("rp2", (128, 512), BF16)
    ystage = sbB("ystage", (80, 2 * 512), BF16)

    ld_consts = []
    for dst, src in ((c_cos4, cos4_h), (c_sin4, sin4_h), (c_mask, mask4_h),
                     (c_idn, idn_h), (c_bias, bias_h)):
        ld_consts.append(P.add("sync", lambda e, d=dst, s=src:
                               e.dma_start(out=d[:], in_=s), inc="S_CONST", n=16))
    t_consts = _mx(*ld_consts)
    gp_up = []
    for dst, srch in ((wupq, wqu_h), (wupk, wku_h), (wupv, wvu_h)):
        for hh in range(2):
            gp_up.append(P.add("gp", lambda e, d=dst, s=srch, hh=hh:
                               e.dma_start(out=d[64 * hh:64 * (hh + 1), :], in_=s),
                               inc="GD", n=16))
    tq_s = P.add("v", lambda e: e.tensor_scalar(
        out=wupq_s[:], in0=wupq[:], scalar1=0.125, scalar2=0.0,
        op0=Alu.mult, op1=Alu.bypass), wait=_mx(*gp_up), inc="V")
    tok_up = _mx(tq_s, *gp_up)
    tok_dbuf_init = P.add("v", lambda e: e.memset(dbuf[64:80, :], 0.0), inc="V")
    tok_ps_init = P.add("v", lambda e: (e.memset(psA[:], 0.0),
                                        e.memset(psB[:], 0.0))[-1], inc="V")

    qset_prev = [None, None]

    def emit_quant(src_ap, dst_ap, set_i, src_wait, dst_wait, out_tok):
        b = [q[:, 512 * set_i:512 * (set_i + 1)] for q in qb]
        bmx = [m[:, 16 * set_i:16 * (set_i + 1)] for m in bm]
        seg = src_ap.rearrange("p (nb s) -> p nb s", s=32)
        w0 = _mx(*src_wait, qset_prev[set_i])
        tV = P.add("v", lambda e, bmx=bmx, seg=seg: e.tensor_reduce(
            out=bmx[0], in_=seg, axis=AXX, op=Alu.max,
            apply_absolute_value=True), wait=w0, inc="V")
        yield
        tV = P.add("v", lambda e, bmx=bmx: e.tensor_scalar(
            out=bmx[1], in0=bmx[0], scalar1=1.0 / 60.0, scalar2=1.25e-31,
            op0=Alu.mult, op1=Alu.max), wait=[tV], inc="V")
        yield
        tV = P.add("v", lambda e, bmx=bmx: e.reciprocal(out=bmx[2], in_=bmx[1]),
                   wait=[tV], inc="V")
        yield
        rsb = bmx[2].rearrange("p nb -> p nb ()").broadcast_to((128, 16, 32))
        s8b = bmx[1].rearrange("p nb -> p nb ()").broadcast_to((128, 16, 32))
        tY = P.add("gp", lambda e, b=b, seg=seg, rsb=rsb: e.tensor_tensor(
            out=b[0].rearrange("p (nb s) -> p nb s", s=32), in0=seg, in1=rsb,
            op=Alu.mult), wait=_mx(tV, w0), inc="G")
        yield
        tSq = P.add("gp", lambda e, b=b: e.tensor_tensor(
            out=b[1], in0=b[0], in1=b[0], op=Alu.mult), wait=[tY], inc="G")
        yield
        tWa = P.add("v", lambda e, b=b: e.tensor_scalar(
            out=b[2], in0=b[1], scalar1=256.0, scalar2=-0.5,
            op0=Alu.is_ge, op1=Alu.mult), wait=[tSq], inc="V")
        yield
        tWb = P.add("v", lambda e, b=b: e.tensor_scalar(
            out=b[3], in0=b[1], scalar1=1024.0, scalar2=-0.25,
            op0=Alu.is_ge, op1=Alu.mult), wait=[tSq], inc="V")
        yield
        tW2 = P.add("v", lambda e, b=b: e.scalar_tensor_tensor(
            out=b[1], in0=b[2], scalar=1.0, in1=b[3],
            op0=Alu.add, op1=Alu.add), wait=[tWa, tWb], inc="V")
        yield
        tR = P.add("gp", lambda e, b=b: e.tensor_tensor(
            out=b[4], in0=b[0], in1=b[1], op=Alu.mult),
            wait=_mx(tY, tW2), inc="G")
        yield
        tIv = P.add("v", lambda e, b=b: e.reciprocal(
            out=b[3], in_=b[1]), wait=_mx(tW2), inc="V")
        yield
        tD = tR
        seq = [(qb[4], qb[5]), (qb[5], qb[1]), (qb[1], qb[5]),
               (qb[5], qb[1]), (qb[1], qb[5])]
        for (s_t, d_t), K in zip(seq, (16.0, 8.0, 4.0, 2.0, 1.0)):
            s_ap = s_t[:, 512 * set_i:512 * (set_i + 1)]
            d_ap = d_t[:, 512 * set_i:512 * (set_i + 1)]
            tD = P.add("v", lambda e, s=s_ap, d=d_ap, K=K: e.add_range_wrap(
                out=d, in_=s, shift=0.0, bound=K - 0.5, period=K),
                wait=[tD], inc="V")
            yield
        tN = P.add("gp", lambda e, b=b: e.tensor_tensor(
            out=b[0], in0=b[4], in1=b[5], op=Alu.subtract),
            wait=_mx(tR, tD), inc="G")
        yield
        tZ = P.add("gp", lambda e, b=b: e.tensor_tensor(
            out=b[4], in0=b[0], in1=b[3], op=Alu.mult),
            wait=_mx(tN, tIv), inc="G")
        yield
        tX = P.add("gp", lambda e, b=b, dst=dst_ap, s8b=s8b: e.tensor_tensor(
            out=dst.rearrange("p (nb s) -> p nb s", s=32),
            in0=b[4].rearrange("p (nb s) -> p nb s", s=32), in1=s8b,
            op=Alu.mult), wait=_mx(tZ, *dst_wait), inc="G")
        yield
        qset_prev[set_i] = tX
        out_tok[0] = tX
        yield

    wf_free = [None, None]
    wd_free = [None, None]
    xnf_free = [None, None]
    xdb_free = [None, None]
    tok_xd_store = []
    qstate = {"qi": 0, "ring": 0}

    def emit_wd_tile(wt):
        r = qstate["ring"] % 2
        qstate["ring"] += 1
        tL = P.add("sync", lambda e, r=r, wt=wt: e.dma_start(
            out=wf_ring[:, C * r:C * (r + 1)],
            in_=wffn_h[128 * wt:128 * (wt + 1), :]),
            wait=_mx(wf_free[r]), inc=f"SWF{r}", n=16)
        tq = []
        for hv in range(2):
            ot = [None]
            yield from emit_quant(
                wf_ring[:, C * r + 512 * hv:C * r + 512 * (hv + 1)],
                wd_ring[:, 512 * hv:512 * (hv + 1)],
                hv, [tL], [wd_free[hv]], ot)
            tq.append(ot[0])
            qstate["qi"] += 1
            tS = P.add("sync", lambda e, wt=wt, hv=hv: e.dma_start(
                out=wd_s[128 * wt:128 * (wt + 1), 512 * hv:512 * (hv + 1)],
                in_=wd_ring[:, 512 * hv:512 * (hv + 1)]),
                wait=_mx(ot[0]), inc="SWDST0", n=16)
            wd_free[hv] = tS
        wf_free[r] = _mx(*tq)

    def emit_quant_tile(t):
        r = qstate["ring"] % 2
        qstate["ring"] += 1
        cc_ = {k: stats[k][:, t:t + 1] for k in ("rstd", "nmr")}
        tL = P.add("sync", lambda e, r=r, t=t: e.dma_start(
            out=wf_ring[:, C * r:C * (r + 1)],
            in_=x_h[128 * t:128 * (t + 1), :]),
            wait=_mx(wf_free[r]), inc=f"SWF{r}", n=16)
        tap = P.add("a", lambda e, r=r, c=cc_: e.activation(
            out=xnf_ring[:, C * r:C * (r + 1)], in_=wf_ring[:, C * r:C * (r + 1)],
            func=Act.Identity, scale=c["rstd"], bias=c["nmr"]),
            wait=_mx(tL, xnf_free[r], tok_stats[t]), inc="A")
        wf_free[r] = tap
        tq = []
        for hv in range(2):
            ot = [None]
            yield from emit_quant(
                xnf_ring[:, C * r + 512 * hv:C * r + 512 * (hv + 1)],
                xdb_ring[:, C * r + 512 * hv:C * r + 512 * (hv + 1)],
                hv, [tap], [xdb_free[r]], ot)
            tq.append(ot[0])
            qstate["qi"] += 1
        xnf_free[r] = _mx(*tq)
        ts2 = P.add("sync", lambda e, r=r, t=t: e.dma_start(
            out=xd_s[128 * t:128 * (t + 1), :],
            in_=xdb_ring[:, C * r:C * (r + 1)]), wait=_mx(*tq),
            inc=f"SXDST{r}", n=16)
        tok_xd_store.append(ts2)
        xdb_free[r] = ts2

    if _MAX_PHASE >= 2:
        x_free = [None] * 3
        xnb_free = [None, None]
        tok_xn_store = []
        tok_stats = {}
        for t in range(16):
            r4, r2 = t % 3, t % 2
            cc_ = {k: stats[k][:, t:t + 1] for k in stats}
            tL = P.add("sync", lambda e, r4=r4, t=t: e.dma_start(
                out=x_ring[:, C * r4:C * (r4 + 1)],
                in_=x_h[128 * t:128 * (t + 1), :]),
                wait=_mx(x_free[r4]), inc=f"SX{r4}", n=16)
            xt = x_ring[:, C * r4:C * (r4 + 1)]
            tS1 = P.add("v", lambda e, xt=xt, c=cc_: e.tensor_reduce(
                out=c["su"], in_=xt, axis=AXX, op=Alu.add), wait=[tL], inc="V")
            tS2 = P.add("v", lambda e, xt=xt, c=cc_, r2=r2: e.tensor_tensor_reduce(
                out=xnb_ring[:, C * r2:C * (r2 + 1)], in0=xt, in1=xt, scale=1.0,
                scalar=0.0, op0=Alu.mult, op1=Alu.add, accum_out=c["sq"]),
                wait=_mx(tL, xnb_free[r2]), inc="V")
            tc = P.add("v", lambda e, c=cc_: e.tensor_scalar(
                out=c["mean"], in0=c["su"], scalar1=1.0 / C, scalar2=0.0,
                op0=Alu.mult, op1=Alu.bypass), wait=[tS1], inc="V")
            tc = P.add("v", lambda e, c=cc_: e.tensor_scalar(
                out=c["ex2"], in0=c["sq"], scalar1=1.0 / C, scalar2=0.0,
                op0=Alu.mult, op1=Alu.bypass), wait=_mx(tS2, tc), inc="V")
            tc = P.add("v", lambda e, c=cc_: e.tensor_tensor(
                out=c["m2"], in0=c["mean"], in1=c["mean"], op=Alu.mult),
                wait=[tc], inc="V")
            tc = P.add("v", lambda e, c=cc_: e.tensor_tensor(
                out=c["var"], in0=c["ex2"], in1=c["m2"], op=Alu.subtract),
                wait=[tc], inc="V")
            tc = P.add("v", lambda e, c=cc_: e.tensor_scalar(
                out=c["ve"], in0=c["var"], scalar1=1e-5, scalar2=0.0,
                op0=Alu.add, op1=Alu.bypass), wait=[tc], inc="V")
            ta = P.add("a", lambda e, c=cc_: e.activation(
                out=c["sd"], in_=c["ve"], func=Act.Sqrt), wait=[tc], inc="A")
            tc = P.add("v", lambda e, c=cc_: e.reciprocal(
                out=c["r0"], in_=c["sd"]), wait=[ta], inc="V")
            tc = P.add("v", lambda e, c=cc_: e.tensor_tensor(
                out=c["t1"], in0=c["r0"], in1=c["r0"], op=Alu.mult),
                wait=[tc], inc="V")
            tc = P.add("v", lambda e, c=cc_: e.tensor_tensor(
                out=c["t2"], in0=c["t1"], in1=c["ve"], op=Alu.mult),
                wait=[tc], inc="V")
            tc = P.add("v", lambda e, c=cc_: e.tensor_scalar(
                out=c["t3"], in0=c["t2"], scalar1=-0.5, scalar2=1.5,
                op0=Alu.mult, op1=Alu.add), wait=[tc], inc="V")
            tc = P.add("v", lambda e, c=cc_: e.tensor_tensor(
                out=c["rstd"], in0=c["r0"], in1=c["t3"], op=Alu.mult),
                wait=[tc], inc="V")
            tc = P.add("v", lambda e, c=cc_: e.tensor_tensor(
                out=c["nmr"], in0=c["mean"], in1=c["rstd"], op=Alu.mult),
                wait=[tc], inc="V")
            tc = P.add("v", lambda e, c=cc_: e.tensor_scalar(
                out=c["nmr"], in0=c["nmr"], scalar1=-1.0, scalar2=0.0,
                op0=Alu.mult, op1=Alu.bypass), wait=[tc], inc="V")
            tok_stats[t] = tc
            tap = P.add("a", lambda e, xt=xt, c=cc_, r2=r2: e.activation(
                out=xnb_ring[:, C * r2:C * (r2 + 1)], in_=xt, func=Act.Identity,
                scale=c["rstd"], bias=c["nmr"]),
                wait=_mx(tc, xnb_free[r2], tS2), inc="A")
            x_free[r4] = tap
            ts_ = P.add("a", lambda e, t=t, r2=r2: e.dma_start(
                out=xn_s[128 * t:128 * (t + 1), :],
                in_=xnb_ring[:, C * r2:C * (r2 + 1)]), wait=[tap],
                inc=f"SXNST{t // 8}", n=16)
            tok_xn_store.append(ts_)
            xnb_free[r2] = ts_
        tok_xn_all = _mx(*tok_xn_store)
        tok_xn_half = [_mx(*tok_xn_store[:8]), _mx(*tok_xn_store[8:])]
        quant_work = ([("wd", wt) for wt in range(8)]
                      + [("q", t) for t in range(16)])
        qptr = [0]
        qgens = []
        qnewest = [None]

        def pump_quant(k):
            while k > 0:
                if ((len(qgens) < 2 and qptr[0] < len(quant_work))
                        and (qnewest[0] is None or qnewest[0][0] >= 21)):
                    kind, idx = quant_work[qptr[0]]
                    qptr[0] += 1
                    cnt = [0]
                    qnewest[0] = cnt
                    qgens.append(((emit_wd_tile if kind == "wd"
                                   else emit_quant_tile)(idx), cnt))
                if not qgens:
                    return
                gen, cnt = qgens.pop(0)
                try:
                    next(gen)
                    cnt[0] += 1
                    qgens.append((gen, cnt))
                    k -= 1
                except StopIteration:
                    pass

    if _MAX_PHASE >= 3:
        psA_guard = [None, None]
        psB_guard = [None] * 4
        tok_y_store = []
        tok_d_all = []
        tok_lat_mm_prev = None
        pring_free = [None, None]
        pm_free = [None, None]
        ystage_free = [None, None]

        txn = {}
        for cc in range(8):
            for th in range(2):
                txn[(cc, th)] = P.add("a", lambda e, cc=cc, th=th:
                                      e.dma_start_transpose(
                    out=xnT[:, T * cc + 1024 * th:T * cc + 1024 * (th + 1)],
                    in_=xn_s[1024 * th:1024 * (th + 1), 128 * cc:128 * (cc + 1)]),
                    wait=_mx(tok_xn_half[th],
                             txn[(cc, 0)] if th else None),
                    inc=f"SXNT{cc}", n=16)
        txn_all = _mx(*txn.values())

        wg_guard = [None, None]
        wcast_tok = {}

        def emit_wload(g):
            sl = g % 2
            wcast_tok[g] = {}
            for w in range(3):
                tL = None
                for cc in range(8):
                    tL = P.add("gp", lambda e, w=w, cc=cc, g=g, sl=sl:
                               e.dma_start(
                        out=wg_bf[w][:, 2048 * sl + 256 * cc:2048 * sl + 256 * (cc + 1)],
                        in_=W_H[w][128 * cc:128 * (cc + 1), 256 * g:256 * (g + 1)]),
                        wait=_mx(wg_guard[sl]), inc=f"SWC{sl}{w}", n=16)
                wcast_tok[g][w] = tL

        emit_wload(0)
        for g in range(_MAX_GROUPS):
            if g + 1 < _MAX_GROUPS:
                emit_wload(g + 1)
            wcast = wcast_tok[g]
            gsl = g % 2

            lat_evac = {}
            for w in range(3):
                for half in range(2):
                    ri = 2 * w + half
                    psX = psA if ri % 2 == 0 else psB
                    mm_tok = [None, None]
                    for t5h in range(2):
                        for cc in range(8):
                            off = 256 * cc + 128 * half

                            def mml(e, w=w, off=off, cc=cc, gsl=gsl, psX=psX,
                                    t5h=t5h):
                                lh = wg_bf[w][:, 2048 * gsl + off:2048 * gsl + off + 128]
                                e.ldweights(lh)
                                last = None
                                for t5 in (2 * t5h, 2 * t5h + 1):
                                    last = e.matmul(
                                        out=psX[:, 512 * t5:512 * (t5 + 1)],
                                        lhsT=lh,
                                        rhs=xnT[:, T * cc + 512 * t5:T * cc + 512 * (t5 + 1)],
                                        start=(cc == 0), stop=(cc == 7))
                                return last
                            guard = (psA_guard if ri % 2 == 0 else psB_guard)
                            mm_tok[t5h] = P.add(
                                "t", mml,
                                wait=_mx(wcast[w], txn[(cc, t5h)],
                                         *(guard if cc == 0 else [])), inc="TE")
                    ev_eng = "v" if ri == 0 else "a"
                    for t5 in range(4):
                        def evl(e, w=w, half=half, t5=t5, psX=psX, ev=ev_eng):
                            dst = lat[w][:, T * half + 512 * t5:T * half + 512 * (t5 + 1)]
                            src = psX[:, 512 * t5:512 * (t5 + 1)]
                            return (e.copy(out=dst, in_=src) if ev == "a"
                                    else e.tensor_copy(out=dst, in_=src))
                        te = P.add(ev_eng, evl, wait=[mm_tok[t5 // 2]],
                                   inc="A" if ev_eng == "a" else "V")
                        lat_evac[(w, half, t5)] = te
                    if ri % 2 == 0:
                        psA_guard = [_mx(lat_evac[(w, half, 0)], lat_evac[(w, half, 1)]),
                                     _mx(lat_evac[(w, half, 2)], lat_evac[(w, half, 3)])]
                    else:
                        psB_guard = [_mx(lat_evac[(w, half, t5)]) for t5 in range(4)]
                    if g or ri:
                        pump_quant(10)
            tok_lat_mm_prev = P.tok("TE")
            wg_guard[gsl] = tok_lat_mm_prev
            lat_all = _mx(*lat_evac.values())
            if _P2_SUB < 2:
                continue

            for side in range(2):
                wup = wupq_s if side == 0 else wupk
                dst = qAB if side == 0 else kAB
                latw = lat[side]
                for t5 in range(4):
                    def mmu(e, latw=latw, wup=wup, t5=t5):
                        last = None
                        for j in range(4):
                            p0 = 64 * (j % 2)
                            pair = j // 2
                            rhs = latw[p0:p0 + 64,
                                       T * (j // 2) + 512 * t5:T * (j // 2) + 512 * (t5 + 1)]
                            wl = wup[p0:p0 + 64, :]
                            mrow = 64 * (j % 2)
                            bce = 512 * pair
                            brot = 512 * (2 + pair)
                            e.ldweights(wl[:, 0:32], tile_position=(p0, mrow))
                            e.matmul(out=psB[mrow:mrow + 32, bce:bce + 512],
                                     lhsT=wl[:, 0:32], rhs=rhs, start=True,
                                     stop=True, tile_position=(p0, mrow))
                            e.ldweights(wl[:, 32:64], tile_position=(p0, mrow + 32))
                            e.matmul(out=psB[mrow + 32:mrow + 64, bce:bce + 512],
                                     lhsT=wl[:, 32:64], rhs=rhs, start=True,
                                     stop=True, tile_position=(p0, mrow + 32))
                            e.ldweights(wl[:, 64:96], tile_position=(p0, mrow + 32))
                            last = e.matmul(out=psB[mrow + 32:mrow + 64, brot:brot + 512],
                                            lhsT=wl[:, 64:96], rhs=rhs, start=True,
                                            stop=True, tile_position=(p0, mrow + 32))
                        return last
                    wl_t = [lat_evac[(side, hh, t5)] for hh in range(2)]
                    mm_tok = P.add("t", mmu,
                                   wait=_mx(*(wl_t + psB_guard + [tok_up, tok_ps_init])),
                                   inc="TE")
                    if g:
                        pump_quant(4)
                    for pair in range(2):
                        bce = 512 * pair
                        brot = 512 * (2 + pair)
                        tm1 = P.add("v", lambda e, bce=bce, t5=t5: e.tensor_tensor(
                            out=rp1[:], in0=psB[:, bce:bce + 512],
                            in1=c_cos4[:, 512 * t5:512 * (t5 + 1)], op=Alu.mult),
                            wait=_mx(mm_tok, t_consts), inc="V")
                        tm2 = P.add("v", lambda e, brot=brot, t5=t5: e.tensor_tensor(
                            out=rp2[:], in0=psB[:, brot:brot + 512],
                            in1=c_sin4[:, 512 * t5:512 * (t5 + 1)], op=Alu.mult),
                            wait=_mx(mm_tok, t_consts), inc="V")
                        tro = P.add("v", lambda e, dst=dst, pair=pair, t5=t5:
                                    e.tensor_tensor(
                            out=dst[pair][:, 512 * t5:512 * (t5 + 1)], in0=rp1[:],
                            in1=rp2[:], op=Alu.add), wait=[tm1, tm2], inc="V")
                        psB_guard[pair] = tm1
                        psB_guard[2 + pair] = _mx(tm2, tro)
            tok_qk = P.tok("V")
            if _P2_SUB < 3:
                continue

            vz = []
            for j in range(4):
                h = 4 * g + j
                tz = P.add("v", lambda e, j=j: e.memset(v_ext[j][:], 0.0),
                           wait=_mx(tok_lat_mm_prev if g else None), inc="V")
                to = P.add("v", lambda e, j=j, h=h: e.memset(
                    v_ext[j][:].rearrange("p (a b) -> p a b", b=80)[:, :, 64 + h:65 + h],
                    1.0), wait=[tz], inc="V")
                vz.append(to)
            for j in range(4):
                for r2 in range(2):
                    def mmv(e, j=j, r2=r2):
                        last = None
                        for k in range(8):
                            tc2 = 8 * r2 + k
                            lhs = lat[2][64 * (j % 2):64 * (j % 2) + 64,
                                         T * (j // 2) + 128 * tc2:T * (j // 2) + 128 * (tc2 + 1)]
                            p0 = 64 * (j % 2)
                            e.ldweights(lhs, tile_position=(p0, 0))
                            last = e.matmul(
                                out=psB[:, 1536 + 64 * k:1536 + 64 * (k + 1)],
                                lhsT=lhs, rhs=wupv[p0:p0 + 64, :], start=True, stop=True,
                                tile_position=(p0, 0))
                        return last
                    mm_tok = P.add("t", mmv,
                                   wait=_mx(lat_all, tok_up, psB_guard[3]),
                                   inc="TE")
                    if g:
                        pump_quant(3)
                    tev = P.add("v", lambda e, j=j, r2=r2: e.tensor_copy(
                        out=v_ext[j][:, 80 * 8 * r2:80 * 8 * (r2 + 1)].rearrange(
                            "p (a b) -> p a b", b=80)[:, :, 0:64],
                        in_=psB[:, 1536:2048].rearrange("p (a b) -> p a b", b=64)),
                        wait=_mx(mm_tok, vz[j]), inc="V")
                    psB_guard[3] = tev
            tok_v = P.tok("V")
            if _P2_SUB < 4:
                continue

            entries = []
            for pa in range(2):
                for qt in range(4):
                    nkc = 4 * qt + 4
                    order = (list(range(4 * qt, 4 * qt + 4))
                             + list(range(0, 4 * qt)))
                    for oi, kc in enumerate(order):
                        entries.append(dict(
                            pa=pa, qt=qt, kc=kc, u=4 * pa + qt,
                            diag=(kc >= 4 * qt), first=(oi == 0),
                            last=(oi == nkc - 1)))
            NE = len(entries)

            def emit_sc(gi):
                en = entries[gi]
                s = gi % 2
                en["s"] = s
                pa, qt, kc = en["pa"], en["qt"], en["kc"]
                a0s = 128 * (kc - 4 * qt) if en["diag"] else 0
                def mmsc(e, pa=pa, qt=qt, kc=kc, s=s, a0s=a0s):
                    l0 = kAB[pa][0:64, 128 * kc:128 * (kc + 1)]
                    l1 = kAB[pa][64:128, 128 * kc:128 * (kc + 1)]
                    e.ldweights(l0, tile_position=(0, 0))
                    e.ldweights(l1, tile_position=(64, 0))
                    e.matmul(out=psA[:, 1024 * s + a0s:1024 * s + 512], lhsT=l0,
                             rhs=qAB[pa][0:64, 512 * qt + a0s:512 * (qt + 1)],
                             start=True, stop=True, tile_position=(0, 0))
                    return e.matmul(
                        out=psA[:, 1024 * s + 512 + a0s:1024 * (s + 1)], lhsT=l1,
                        rhs=qAB[pa][64:128, 512 * qt + a0s:512 * (qt + 1)],
                        start=True, stop=True, tile_position=(64, 0))
                tsc = P.add("t", mmsc, wait=_mx(tok_qk, psA_guard[s]), inc="TE")
                if en["diag"]:
                    pm = gi % 2
                    m = kc - 4 * qt
                    en["a0"] = 128 * m
                    tE = P.add("a", lambda e, s=s, pm=pm: e.activation(
                        out=p_m[:, 1024 * pm:1024 * (pm + 1)],
                        in_=psA[:, 1024 * s:1024 * (s + 1)], func=Act.Exp),
                        wait=_mx(tsc, pm_free[pm]), inc="A")
                    psA_guard[s] = tE
                    tMs = []
                    for jj in range(2):
                        o0 = 1024 * pm + 512 * jj + 128 * m
                        tMs.append(P.add("v", lambda e, o0=o0: e.tensor_tensor(
                            out=p_m[:, o0:o0 + 128],
                            in0=p_m[:, o0:o0 + 128],
                            in1=c_mask[:, 0:128], op=Alu.mult),
                            wait=_mx(tE, t_consts), inc="V"))
                    en["tok"], en["pm"] = _mx(*tMs), pm
                else:
                    en["a0"] = 0
                    tE = P.add("a", lambda e, s=s: e.activation(
                        out=p_ring[:, 1024 * s:1024 * (s + 1)],
                        in_=psA[:, 1024 * s:1024 * (s + 1)], func=Act.Exp),
                        wait=_mx(tsc, pring_free[s]), inc="A")
                    psA_guard[s] = tE
                    en["tok"], en["pm"] = tE, None

            def emit_pv(gi):
                en = entries[gi]
                s2, pm2 = en["s"], en["pm"]
                pa, qt, kc, u = en["pa"], en["qt"], en["kc"], en["u"]
                first, last_kc = en["first"], en["last"]
                a0 = en["a0"]
                def mmpv(e, kc=kc, s2=s2, pm2=pm2, pa=pa, a0=a0,
                         first=first, last_kc=last_kc):
                    src_t = p_m if pm2 is not None else p_ring
                    off = 1024 * (pm2 if pm2 is not None else s2)
                    last = None
                    for combo in (((0, 0), (1, 1)), ((1, 0), (0, 1))):
                        lhs = []
                        for jj, hh in combo:
                            j = 2 * pa + jj
                            p0 = 64 * hh
                            lh = v_ext[j][p0:p0 + 64, 80 * kc:80 * (kc + 1)]
                            e.ldweights(lh, tile_position=(p0, 0))
                            lhs.append((jj, hh, p0, lh))
                        for jj, hh, p0, lh in lhs:
                            src = src_t[p0:p0 + 64,
                                        off + 512 * jj + a0:off + 512 * (jj + 1)]
                            bk0 = 512 * (2 * jj + hh)
                            last = e.matmul(
                                out=psB[0:80, bk0 + a0:bk0 + 512],
                                lhsT=lh, rhs=src, start=first, stop=last_kc,
                                tile_position=(p0, 0))
                    return last
                wpv = _mx(en["tok"], tok_v,
                          _mx(*psB_guard) if first else None)
                tpv = P.add("t", mmpv, wait=wpv, inc="TE")
                if pm2 is not None:
                    pm_free[pm2] = tpv
                else:
                    pring_free[s2] = tpv
                if last_kc:
                    for jj in range(2):
                        h = 4 * g + 2 * pa + jj
                        bkA, bkB = 2 * jj, 2 * jj + 1
                        r2 = (2 * u + jj) % 2
                        tyc = P.add("a", lambda e, bkA=bkA, r2=r2: e.copy(
                            out=ystage[:, 512 * r2:512 * (r2 + 1)],
                            in_=psB[0:80, 512 * bkA:512 * (bkA + 1)]),
                            wait=_mx(tpv, ystage_free[r2]), inc="A")
                        tye = P.add("v", lambda e, bkB=bkB, r2=r2:
                                    e.tensor_tensor(
                            out=ystage[:, 512 * r2:512 * (r2 + 1)],
                            in0=ystage[:, 512 * r2:512 * (r2 + 1)],
                            in1=psB[0:80, 512 * bkB:512 * (bkB + 1)],
                            op=Alu.add),
                            wait=[tyc], inc="V")
                        tde = P.add("v", lambda e, qt=qt, r2=r2:
                                    e.tensor_tensor(
                            out=dbuf[64:80, 512 * qt:512 * (qt + 1)],
                            in0=dbuf[64:80, 512 * qt:512 * (qt + 1)],
                            in1=ystage[64:80, 512 * r2:512 * (r2 + 1)],
                            op=Alu.add),
                            wait=_mx(tye, tok_dbuf_init), inc="V")
                        tok_d_all.append(tde)
                        tst = P.add("sync", lambda e, qt=qt, h=h, r2=r2:
                                    e.dma_start(
                            out=y_s[64 * h:64 * (h + 1), 512 * qt:512 * (qt + 1)],
                            in_=ystage[0:64, 512 * r2:512 * (r2 + 1)]),
                            wait=[tye], inc=f"SYS{r2}", n=16)
                        tok_y_store.append(tst)
                        ystage_free[r2] = _mx(tst, tde)
                        psB_guard[bkA] = _mx(tyc)
                        psB_guard[bkB] = _mx(tye)

            for gi in range(NE + 2):
                if gi >= 2:
                    emit_pv(gi - 2)
                if gi < NE:
                    emit_sc(gi)
                if not (g == 0 and gi < 12):
                    pump_quant(2)


        pump_quant(10 ** 9)
        tok_y_all = _mx(*tok_y_store)
        tok_xd_all = _mx(*tok_xd_store)
        tok_wd_done = _mx(P.tok("SWDST0"))

    if _MAX_PHASE >= 4:
        trec = P.add("v", lambda e: e.reciprocal(out=dbuf[64:80, :],
                                                 in_=dbuf[64:80, :]),
                     wait=_mx(*tok_d_all), inc="V")

        for cm in reversed(cmsB):
            cm.__exit__(None, None, None)
        cmsB.clear()
        wdT = sb("wdT", (128, 8 * C), BF16)
        xdT = sb("xdT", (128, 2 * 8 * 512), BF16)
        yn = sb("yn", (128, 8 * T), BF16)
        yld = sb("yld", (128, 2 * T), BF16)
        wc_bf = sb("wc_bf", (128, 8 * C), BF16)
        ffn_stage = sb("ffn_stage", (128, 2 * 512), BF16)
        ffn_ld = sb("ffn_ld", (128, 2 * 512), BF16)
        tmpc = sb("tmpc", (128, 2 * 512), F32)
        out_sb = sb("out_sb", (128, 4 * C), F32)
        x4_extra = sb("x4_extra", (128, C), F32)
        c_ph = sb("c_ph", (80, 1024), F32)

        barrier = _mx(*[(s_, c_) for s_, c_ in P.count.items()])
        t_ph = P.add("sync", lambda e: e.dma_start(out=c_ph[:], in_=ph_h),
                     wait=_mx(barrier), inc="S_CONST", n=16)

        twdT = []
        for cc in range(8):
            twdT.append(P.add("a",
                              lambda e, cc=cc: e.dma_start_transpose(
                out=wdT[:, C * cc:C * (cc + 1)],
                in_=wd_s[:, 128 * cc:128 * (cc + 1)]),
                wait=_mx(tok_wd_done, tok_xd_all), inc="SWDT", n=16))
        xdT_free = [None, None]
        ffn_stage_free = [None, None]
        ffn_store = {}
        psA_free2 = [barrier, barrier]
        for t5 in range(4):
            r = t5 % 2
            txd = []
            for cc in range(8):
                txd.append(P.add("a",
                                 lambda e, cc=cc, t5=t5, r=r:
                                 e.dma_start_transpose(
                    out=xdT[:, 512 * (8 * r + cc):512 * (8 * r + cc + 1)],
                    in_=xd_s[512 * t5:512 * (t5 + 1), 128 * cc:128 * (cc + 1)]),
                    wait=_mx(tok_xd_all, xdT_free[r]), inc=f"SXDT{r}", n=16))
            last_mm = None
            for co in range(8):
                b = co % 2

                def mmf(e, co=co, r=r):
                    last = None
                    for cc in range(8):
                        e.ldweights(wdT[:, C * cc + 128 * co:C * cc + 128 * (co + 1)])
                        last = e.matmul(
                            out=psA[:, 512 * (co % 2):512 * (co % 2 + 1)],
                            lhsT=wdT[:, C * cc + 128 * co:C * cc + 128 * (co + 1)],
                            rhs=xdT[:, 512 * (8 * r + cc):512 * (8 * r + cc + 1)],
                            start=(cc == 0), stop=(cc == 7))
                    return last
                tmm = P.add("t", mmf, wait=_mx(*(txd + twdT + [psA_free2[b]])),
                            inc="TE")
                tev = P.add("a", lambda e, co=co, b=b: e.activation(
                    out=ffn_stage[:, 512 * b:512 * (b + 1)],
                    in_=psA[:, 512 * b:512 * (b + 1)], func=Act.Identity,
                    scale=1.0, bias=c_bias[:, co:co + 1]),
                    wait=_mx(tmm, ffn_stage_free[b]), inc="A")
                psA_free2[b] = tev
                tst = P.add("sync", lambda e, co=co, t5=t5, b=b: e.dma_start(
                    out=ffn_s[128 * co:128 * (co + 1), 512 * t5:512 * (t5 + 1)],
                    in_=ffn_stage[:, 512 * b:512 * (b + 1)]),
                    wait=[tev], inc=f"SFST{b}", n=16)
                ffn_stage_free[b] = tst
                ffn_store[(co, t5)] = tst
                last_mm = tmm
            xdT_free[r] = last_mm

        yld_free = [None, None]
        psB_free3 = [barrier] * 4
        tyn = {}
        for cc in range(8):
            r = cc % 2
            tly = P.add(("a", "sync")[r], lambda e, cc=cc, r=r: e.dma_start(
                out=yld[:, T * r:T * (r + 1)], in_=y_s[128 * cc:128 * (cc + 1), :]),
                wait=_mx(tok_y_all, yld_free[r], barrier), inc=f"SYLD{r}", n=16)
            for t5 in range(4):
                bb = t5 % 4

                def mmb(e, cc=cc, t5=t5, bb=bb):
                    return e.matmul(out=psB[:, 512 * bb:512 * (bb + 1)],
                                    lhsT=c_ph[64:80, 128 * cc:128 * (cc + 1)],
                                    rhs=dbuf[64:80, 512 * t5:512 * (t5 + 1)],
                                    start=True, stop=True)
                tbm = P.add("t", mmb, wait=_mx(trec, t_ph, psB_free3[bb],
                                               barrier), inc="TE")
                tn = P.add("v", lambda e, cc=cc, t5=t5, r=r, bb=bb: e.tensor_tensor(
                    out=yn[:, T * cc + 512 * t5:T * cc + 512 * (t5 + 1)],
                    in0=yld[:, T * r + 512 * t5:T * r + 512 * (t5 + 1)],
                    in1=psB[:, 512 * bb:512 * (bb + 1)], op=Alu.mult),
                    wait=_mx(tly, tbm), inc="V")
                psB_free3[bb] = tn
                tyn[(cc, t5)] = tn
            yld_free[r] = tyn[(cc, 3)]
        twc = {}
        for cc in range(8):
            twc[cc] = P.add("gp", lambda e, cc=cc: e.dma_start(
                out=wc_bf[:, C * cc:C * (cc + 1)],
                in_=wc_h[128 * cc:128 * (cc + 1), :]),
                wait=_mx(barrier), inc="SWC0", n=16)
        twc_all = _mx(*twc.values())
        tyn_all = _mx(*tyn.values())

        x2_free = [barrier] * 4
        xtile = lambda tt: x4_extra[:] if tt == 3 else x_ring[:, C * tt:C * (tt + 1)]
        out_free = [None] * 4
        ffn_ld_free = [None, None]
        tmp_free = [None, None]
        psA_free3 = list(psA_free2)
        psB_free4 = [psB_free3[0], psB_free3[1]]
        for t5 in range(4):
            tx = {}
            for tt in range(4):
                t128 = 4 * t5 + tt
                tx[tt] = P.add("sync", lambda e, t128=t128, tt=tt: e.dma_start(
                    out=xtile(tt),
                    in_=x_h[128 * t128:128 * (t128 + 1), :]),
                    wait=_mx(x2_free[tt], barrier), inc=f"SX{tt}", n=16)
            tadds = {tt: [] for tt in range(4)}
            for co in range(8):
                b = co % 2

                def mmw(e, co=co, t5=t5, b=b):
                    last = None
                    for cc in range(8):
                        e.ldweights(wc_bf[:, C * cc + 128 * co:C * cc + 128 * (co + 1)])
                        last = e.matmul(
                            out=psA[:, 512 * b:512 * (b + 1)],
                            lhsT=wc_bf[:, C * cc + 128 * co:C * cc + 128 * (co + 1)],
                            rhs=yn[:, T * cc + 512 * t5:T * cc + 512 * (t5 + 1)],
                            start=(cc == 0), stop=(cc == 7))
                    return last
                tmm = P.add("t", mmw,
                            wait=_mx(*([tyn[(cc, t5)] for cc in range(8)]
                                       + [twc_all, psA_free3[b]])),
                            inc="TE")
                tlf = P.add("gp", lambda e, co=co, t5=t5, b=b: e.dma_start(
                    out=ffn_ld[:, 512 * b:512 * (b + 1)],
                    in_=ffn_s[128 * co:128 * (co + 1), 512 * t5:512 * (t5 + 1)]),
                    wait=_mx(ffn_store[(co, t5)], ffn_ld_free[b]), inc=f"SFLD{b}", n=16)
                tcb = P.add("v", lambda e, b=b: e.tensor_tensor(
                    out=tmpc[:, 512 * b:512 * (b + 1)],
                    in0=psA[:, 512 * b:512 * (b + 1)],
                    in1=ffn_ld[:, 512 * b:512 * (b + 1)], op=Alu.add),
                    wait=_mx(tmm, tlf, tmp_free[b]), inc="V")
                psA_free3[b] = tcb
                ffn_ld_free[b] = tcb
                last_add = None
                for tt in range(4):
                    pb = (co * 4 + tt) % 2

                    def mtr(e, b=b, tt=tt, pb=pb):
                        return e.transpose(
                            out=psB[:, 512 * pb:512 * pb + 128],
                            in_=tmpc[:, 512 * b + 128 * tt:512 * b + 128 * (tt + 1)],
                            identity=c_idn[:])
                    ttr = P.add("t", mtr, wait=_mx(tcb, psB_free4[pb], t_consts),
                                inc="TE")
                    tad = P.add("v", lambda e, co=co, tt=tt, pb=pb: e.tensor_tensor(
                        out=out_sb[:, C * tt + 128 * co:C * tt + 128 * (co + 1)],
                        in0=psB[:, 512 * pb:512 * pb + 128],
                        in1=xtile(tt)[:, 128 * co:128 * (co + 1)],
                        op=Alu.add), wait=_mx(ttr, tx[tt], out_free[tt]), inc="V")
                    psB_free4[pb] = tad
                    tadds[tt].append(tad)
                    last_add = tad
                tmp_free[b] = last_add
            for tt in range(4):
                t128 = 4 * t5 + tt
                tst = P.add("sync", lambda e, t128=t128, tt=tt: e.dma_start(
                    out=out_h[128 * t128:128 * (t128 + 1), :],
                    in_=out_sb[:, C * tt:C * (tt + 1)]),
                    wait=_mx(*tadds[tt]), inc=f"SOUT{tt}", n=16)
                out_free[tt] = tst
                x2_free[tt] = _mx(*tadds[tt])

    P.add("sync", lambda e: None,
          wait=[(f"SOUT{tt}", P.count.get(f"SOUT{tt}", 0)) for tt in range(4)])

    used = set(P.count)
    for strm in P.streams.values():
        for waits_, _, _, _ in strm:
            used.update(s for s, _ in waits_)
    sems = {s: ent(nc.semaphore(s)) for s in sorted(used)}
    block = ent(nc.Block())

    @block.sync
    def _(e):
        P.realize("sync", e, sems)

    @block.gpsimd
    def _(e):
        P.realize("gp", e, sems)

    @block.vector
    def _(e):
        P.realize("v", e, sems)

    @block.scalar
    def _(e):
        P.realize("a", e, sems)

    @block.tensor
    def _(e):
        P.realize("t", e, sems)

    for cm in reversed(cmsB):
        cm.__exit__(None, None, None)
    for cm in reversed(cms):
        cm.__exit__(None, None, None)
    return nc


def kernel(**inputs):
    in_maps = _layout_inputs(inputs)
    nc = build(taps=_DEBUG_TAPS)
    mybir.codegen_inst_isa_subclasses(nc)
    res = run_bass_kernel_spmd(nc, in_maps, core_ids=list(range(NCORES)))
    out = np.stack([np.asarray(r["out"]) for r in res.results], axis=0)
    if _DEBUG_TAPS:
        kernel._taps = [{k: np.asarray(r[k]) for k in _DEBUG_TAPS}
                        for r in res.results]
    return out
